# version 15
# baseline (speedup 1.0000x reference)
import numpy as np
import ml_dtypes
from contextlib import ExitStack

import concourse.bass as bass
import concourse.mybir as mybir
from concourse.bass_utils import run_bass_kernel_spmd

F32 = mybir.dt.float32
BF16 = mybir.dt.bfloat16
OP = mybir.AluOpType
AF = mybir.ActivationFunctionType

NC = 8
T = 2048
D = 2048
NH = 16
DENOM = float(np.sqrt(2048.0) / 8192.0)
EPS = 1e-5

G_OWN = [(0, 6), (6, 12), (12, 17), (17, 22), (22, 28), (28, 33), (33, 38), (38, 43)]
GPAD = 6
REG_G0 = 0
REG_U0 = GPAD * 2048
REG_D0 = 2 * GPAD * 2048
REG_COLS = REG_D0 + 2 * 5504
CHUNKS = [(0, 4096), (4096, 8192), (8192, 12288),
          (12288, 16384), (16384, 20480), (20480, 24576),
          (24576, 30080), (30080, 35584)]
NCHUNK = len(CHUNKS)
CC_DONE = {1: 2, 2: 3, 3: 4, 4: 7, 5: 10, 6: 11, 7: 12, 8: 13,
           "ca": 5, "cb": 8}

SWEEPS = [list(range(0, 15)), list(range(15, 29)), list(range(29, 43))]

def _gate_loc(ot):
    for c, (a, b) in enumerate(G_OWN):
        if a <= ot < b:
            return c, (ot - a) * 4096
    raise AssertionError(ot)

def _up_loc(ot):
    c, col = _gate_loc(ot)
    return c, col + 2048

def _down_loc(ft):
    return ft // 2, REG_D0 + (ft % 2) * 5504

def _chunk_of_col(col):
    for j, (a, b) in enumerate(CHUNKS):
        if a <= col < b:
            return j + 1
    raise AssertionError(col)

def _mk_pieces():
    out = []
    for j, (a, b) in enumerate(CHUNKS):
        col = a
        while col < b:
            w = min(1024, b - col)
            out.append((j + 1, col, w))
            col += w
    return out

PIECES = _mk_pieces()
NPIECE = len(PIECES)
PIECE_CUM = [min(20, int(np.ceil((et + 1) * 1.5))) for et in range(16)]
CHUNK_CLOSE_PIECE = {}
for _pi, (_j, _c, _w) in enumerate(PIECES):
    if _pi + 1 == NPIECE or PIECES[_pi + 1][0] != _j:
        CHUNK_CLOSE_PIECE[_j] = _pi

def W_XF(h):       return 16 * (h // 4 + 1)
def W_XF2(h):      return 16 * (4 + h // 2 + 1)
def W_K(et):       return 16 * (et // 2 + 1)
def W_Q(et):       return 16 * (8 + et // 2 + 1)
def W_V(et):       return 16 * (et // 2 + 1)
def W_WOC(ft):     return 16 * (8 + ft // 2 + 1)
def W_XRL(ft):     return 16 * (ft + 1)
def W_YRL(s, ft):  return 16 * (16 + 16 * s + ft + 1)
def W_G(ot):       return 16 * (ot // 2 + 1)
def W_U(ot):       return 16 * (ot // 2 + 1)
def W_D(s, ft):    return 16 * ((16 * s + ft) // 2 + 1)
def W_IDX(j):      return 16 * ((j - 1) // 2 + 1)
W_CONST = 16 * 2
def W_STCC(j):     return 16 * j
def W_YA(ft):      return 16 * (ft + 1)
def W_YD(s, ft):   return 16 * (16 + 16 * s + ft + 1)
STY_FINAL = 16 * 64

def MM_STAT(h):
    return h + 1

def MM_BCM(i):
    return 17 + i

MM_OFF = 24

def MM_K(et, tc):
    return MM_OFF + 8 * et + tc + 1

def MM_V(et, tc):
    return MM_OFF + 8 * et + 4 + tc + 1

MM_A_END = MM_OFF + 128

def MM_Q(et, tc):
    return MM_A_END + 4 * et + tc + 1

MM_Q_END = MM_A_END + 64

def MM_WO(ft, tc):
    return MM_Q_END + 4 * ft + tc + 1

MM_WO_END = MM_Q_END + 64

def _mm_ffn_plan():
    val = MM_WO_END
    plan = {}
    for s, sw in enumerate(SWEEPS):
        for ot in sw:
            for tag in ("g1", "u1", "g2", "u2"):
                val += 1
                plan[(tag, ot)] = val
        for ft in range(16):
            for tc in range(4):
                val += 1
                plan[("d", s, ft, tc)] = val
    return plan

MM_FFN = _mm_ffn_plan()

def _ac_plan():
    val = 0
    plan = {}
    def put(k):
        nonlocal val
        val += 1
        plan[k] = val
    for h in range(16):
        put(("sq", h))
    put(("rstd",))
    for et in range(16):
        for tc in range(4):
            put(("kev", et, tc))
    for s, sw in enumerate(SWEEPS):
        for ot in sw:
            put(("silu", ot, 1))
            put(("silu", ot, 2))
    return plan

AC = _ac_plan()

def _ve_plan():
    val = 0
    plan = {}
    def put(k):
        nonlocal val
        val += 1
        plan[k] = val
    put(("init",))
    for h in range(16):
        put(("cast", h))
    put(("smath",))
    put(("scast",))
    put(("bevac",))
    for h in range(16):
        put(("xn", h))
    pi = 0
    for et in range(16):
        for tc in range(4):
            put(("u", et, tc))
        put(("scan", et))
        if et == 7:
            put(("s16a",))
        if et == 15:
            put(("s16b",))
        while pi < PIECE_CUM[et]:
            j = PIECES[pi][0]
            if CHUNK_CLOSE_PIECE[j] == pi:
                put(("selchunk", j))
            pi += 1
        if et == 14:
            put(("carrya",))
    for et in range(16):
        if et == 8:
            put(("carryb",))
        for tc in range(4):
            put(("vhat", et, tc))
        if 2 <= et <= 11 and pi < NPIECE:
            j = PIECES[pi][0]
            if CHUNK_CLOSE_PIECE[j] == pi:
                put(("selchunk", j))
            pi += 1
    for ft in range(16):
        for tc in range(4):
            put(("woev", ft, tc))
        if ft < 6 and pi < NPIECE:
            j = PIECES[pi][0]
            if CHUNK_CLOSE_PIECE[j] == pi:
                put(("selchunk", j))
            pi += 1
    for s, sw in enumerate(SWEEPS):
        for ot in sw:
            put(("hm", ot, 1))
            put(("hm", ot, 2))
        for ft in range(16):
            for tc in range(4):
                put(("dev", s, ft, tc))
    return plan

VE = _ve_plan()


def _build_nc():
    nc = bass.Bass("TRN2", target_bir_lowering=False, debug=False)

    xft_h = nc.dram_tensor("xft_h", [16, 128, T], F32, kind="ExternalInput").ap()
    wq_h = nc.dram_tensor("wq_h", [16, 128, D], F32, kind="ExternalInput").ap()
    wk_h = nc.dram_tensor("wk_h", [16, 128, D], F32, kind="ExternalInput").ap()
    wv_h = nc.dram_tensor("wv_h", [16, 128, D], F32, kind="ExternalInput").ap()
    wo_h = nc.dram_tensor("wo_h", [16, 128, D], F32, kind="ExternalInput").ap()
    idx_h = nc.dram_tensor("idx_h", [128, REG_COLS], BF16, kind="ExternalInput").ap()
    lutb_h = nc.dram_tensor("lutb_h", [128, 16], F32, kind="ExternalInput").ap()
    mask_h = nc.dram_tensor("mask_h", [128, 8], F32, kind="ExternalInput").ap()
    y_h = nc.dram_tensor("y_h", [16, 128, T], F32, kind="ExternalOutput").ap()

    cin, cout = [], []
    for j in range(1, NCHUNK + 1):
        a, b = CHUNKS[j - 1]
        cin.append(nc.dram_tensor(f"cc_in{j}", [128, b - a], BF16, kind="Internal").ap())
        cout.append(nc.dram_tensor(f"cc_out{j}", [NC, 128, b - a], BF16,
                                   kind="Internal", addr_space="Local").ap())
    cs_in_a = nc.dram_tensor("cs_in_a", [128, 8], F32, kind="Internal").ap()
    cs_out_a = nc.dram_tensor("cs_out_a", [NC, 128, 8], F32, kind="Internal",
                              addr_space="Local").ap()
    cs_in_b = nc.dram_tensor("cs_in_b", [128, 8], F32, kind="Internal").ap()
    cs_out_b = nc.dram_tensor("cs_out_b", [NC, 128, 8], F32, kind="Internal",
                              addr_space="Local").ap()
    fl_in = nc.dram_tensor("fl_in", [1, 16], F32, kind="Internal").ap()
    fl_out = [nc.dram_tensor(f"fl_out{i}", [NC, 1, 16], F32, kind="Internal",
                             addr_space="Local").ap() for i in range(3)]
    wkb_d = nc.dram_tensor("wkb_d", [16, 128, D], BF16, kind="Internal").ap()
    wvb_d = nc.dram_tensor("wvb_d", [16, 128, D], BF16, kind="Internal").ap()
    wqb_d = nc.dram_tensor("wqb_d", [16, 128, D], BF16, kind="Internal").ap()
    wob_d = nc.dram_tensor("wob_d", [16, 128, D], BF16, kind="Internal").ap()

    es = ExitStack()
    def sbt(name, shape, dtype):
        return es.enter_context(nc.sbuf_tensor(name, shape, dtype))

    xn = sbt("xn", [128, NH * T], BF16)
    work = sbt("work", [128, 16 * T], BF16)
    big = sbt("big", [128, 31872], BF16)
    wkvb = sbt("wkvb", [128, 4 * D], BF16)
    s16 = sbt("s16", [128, 16], F32)
    sg = sbt("sg", [128, NC * 16], F32)
    ctmp = sbt("ctmp", [128, 128], F32)
    maskb = sbt("maskb", [128, 8], F32)
    lutb = sbt("lutb", [128, 16], F32)
    onesb = sbt("onesb", [128, 128], BF16)

    def bigv(c0, cols, dtype=BF16):
        apb = big[:, c0:c0 + cols]
        return apb if dtype == BF16 else apb.bitcast(dtype)

    sqb = [bigv(0, 2048), bigv(2048, 2048),
           bigv(4096, 2048), bigv(6144, 2048)]
    t1B = bigv(4096, 4096, F32)
    mneg = bigv(4096, 4096, F32)[0:1, :]
    xftv = [bigv(8192, 4096, F32), bigv(12288, 4096, F32),
            bigv(16384, 4096, F32), bigv(20480, 4096, F32)]
    mstat_bf = bigv(16384, 2048)[0:1, :]
    t1A = bigv(16384, 4096, F32)
    varr = bigv(24576, 4096, F32)[0:1, :]
    mbv = bigv(20480, 4096, F32)
    rstdr = bigv(20480, 4096, F32)[0:1, :]
    rbv = bigv(24576, 4096, F32)
    rstat_bf = bigv(18432, 2048)[0:1, :]
    selidx = [bigv(0, 5504), bigv(5504, 5504)]
    selw = bigv(11008, 5504)
    seltmp = bigv(16512, 1024)
    selacc = [bigv(17536, 1024), bigv(18560, 1024)]
    sgst = bigv(16512, 2048)
    kdsb = [bigv(19584, 2048), bigv(21632, 2048)]
    spare = bigv(23680, 2048)
    vtile0 = bigv(19584, 2048)
    vtile1 = bigv(21632, 2048)
    xrlA = bigv(23680, 4096, F32)
    xrlB = work[:, 14 * T:16 * T].bitcast(F32)
    xrlC = bigv(19584, 4096, F32)
    ystgf = bigv(27776, 4096, F32)
    gbuf = [bigv(0, 2048), bigv(2048, 2048)]
    ubuf = [bigv(4096, 2048), bigv(6144, 2048)]
    dbuf = [bigv(8192, 1920), bigv(10112, 1920)]

    banks = [es.enter_context(nc.psum_tensor(f"ps{i}", [128, 512], F32))
             for i in range(8)]

    def sem(name):
        return es.enter_context(nc.semaphore(name))
    s_xf = [sem("s_xf0"), sem("s_xf1"), sem("s_xf2"), sem("s_xf3")]
    s_k = [sem("s_k0"), sem("s_k1")]
    s_v = [sem("s_v0"), sem("s_v1")]
    s_xr = sem("s_xr")
    s_g = [sem("s_g0"), sem("s_g1")]
    s_u = [sem("s_u0"), sem("s_u1")]
    s_d = [sem("s_d0"), sem("s_d1")]
    s_idx = [sem("s_idx0"), sem("s_idx1")]
    s_ccsa, s_ccsb, s_const = sem("s_ccsa"), sem("s_ccsb"), sem("s_const")
    s_stcc, s_sts, s_flst = sem("s_stcc"), sem("s_sts"), sem("s_flst")
    s_ck, s_cv, s_cq, s_co = sem("s_ck"), sem("s_cv"), sem("s_cq"), sem("s_co")
    s_sty = sem("s_sty")
    s_cc, s_mm, s_ve, s_ac = sem("s_cc"), sem("s_mm"), sem("s_ve"), sem("s_ac")
    s_gp = sem("s_gp")

    def vslot(et):
        return ("v0" if et == 0 else ("v1" if et == 1 else et - 2))

    def wslot(i, tc=None):
        if i == "v0":
            base, c0 = vtile0, 0
        elif i == "v1":
            base, c0 = vtile1, 0
        elif i < 0:
            base, c0 = spare, 0
        else:
            base, c0 = work, i * T
        if tc is None:
            return base[:, c0:c0 + T]
        return base[:, c0 + tc * 512: c0 + (tc + 1) * 512]

    def t1slot(h):
        return t1A if h % 2 == 0 else t1B

    def xrl_wo(ft):
        return xrlA if ft % 2 == 0 else xrlB

    def xrl_ffn(ft):
        return xrlA if ft % 2 == 0 else xrlC

    with es:
        block = es.enter_context(nc.Block())

        @block.vector
        def _(v):
            cnt = [0]
            def vinc(inst, key):
                cnt[0] += 1
                assert VE[key] == cnt[0], (key, VE[key], cnt[0])
                inst.then_inc(s_ve, 1)
            vinc(v.memset(onesb[:, :], 1.0), ("init",))
            for h in range(16):
                v.wait_ge(s_xf[h % 4], W_XF(h))
                vinc(v.tensor_copy(xn[:, h * T:(h + 1) * T], xftv[h % 4][:, :]),
                     ("cast", h))
            v.wait_ge(s_mm, MM_STAT(15))
            for tc in range(4):
                v.tensor_scalar(out=mneg[:, tc * 512:(tc + 1) * 512],
                                in0=banks[tc][0:1, :],
                                scalar1=-1.0 / D, scalar2=None, op0=OP.mult)
            v.drain()
            v.tensor_tensor(out=rstdr[:, :], in0=mneg[:, :], in1=mneg[:, :],
                            op=OP.mult)
            for tc in range(4):
                v.tensor_scalar(out=varr[:, tc * 512:(tc + 1) * 512],
                                in0=banks[4 + tc][0:1, :],
                                scalar1=1.0 / D, scalar2=EPS,
                                op0=OP.mult, op1=OP.add)
            v.drain()
            vinc(v.tensor_tensor(out=varr[:, :], in0=varr[:, :], in1=rstdr[:, :],
                                 op=OP.subtract), ("smath",))
            v.wait_ge(s_ac, AC[("rstd",)])
            v.tensor_copy(mstat_bf[:, :], mneg[:, :])
            v.tensor_copy(rstat_bf[:, :], rstdr[:, :])
            vinc(v.drain(), ("scast",))
            for tc in range(4):
                v.wait_ge(s_mm, MM_BCM(tc))
                v.tensor_copy(mbv[:, tc * 512:(tc + 1) * 512], banks[tc][:, :])
            for tc in range(4):
                v.wait_ge(s_mm, MM_BCM(4 + tc))
                v.reciprocal(rbv[:, tc * 512:(tc + 1) * 512], banks[4 + tc][:, :])
            vinc(v.drain(), ("bevac",))
            for h in range(16):
                v.wait_ge(s_gp, h + 1)
                vinc(v.tensor_tensor(out=xn[:, h * T:(h + 1) * T],
                                     in0=t1slot(h)[:, :], in1=rbv[:, :],
                                     op=OP.mult), ("xn", h))
            v.drain()
            def sel_piece(pi):
                j, col, w = PIECES[pi]
                a, b = CHUNKS[j - 1]
                slot = (j - 1) % 2
                ioff = col - a
                v.wait_ge(s_idx[slot], W_IDX(j))
                if col == a and j > 1:
                    v.wait_ge(s_stcc, W_STCC(j - 1))
                for l in range(16):
                    if l == 0:
                        v.tensor_scalar(out=selacc[0][:, 0:w],
                                        in0=selidx[slot][:, ioff:ioff + w],
                                        scalar1=0.0, scalar2=lutb[:, 0:1],
                                        op0=OP.is_equal, op1=OP.mult)
                        v.drain()
                    else:
                        v.tensor_scalar(out=seltmp[:, 0:w],
                                        in0=selidx[slot][:, ioff:ioff + w],
                                        scalar1=float(l), scalar2=lutb[:, l:l + 1],
                                        op0=OP.is_equal, op1=OP.mult)
                        v.drain()
                        v.tensor_tensor(out=selacc[l % 2][:, 0:w],
                                        in0=selacc[1 - (l % 2)][:, 0:w],
                                        in1=seltmp[:, 0:w], op=OP.add)
                        v.drain()
                v.tensor_copy(selw[:, ioff:ioff + w], selacc[1][:, 0:w])
                if CHUNK_CLOSE_PIECE[j] == pi:
                    vinc(v.drain(), ("selchunk", j))
                else:
                    v.drain()
            pi_done = [0]
            for et in range(16):
                for tc in range(4):
                    v.wait_ge(s_mm, MM_V(et, tc))
                    v.wait_ge(s_ac, AC[("kev", et, tc)])
                    vinc(v.scalar_tensor_tensor(
                        out=wslot(et, tc), in0=banks[4 + tc][:, :], scalar=DENOM,
                        in1=kdsb[et % 2][:, tc * 512:(tc + 1) * 512],
                        op0=OP.mult, op1=OP.mult), ("u", et, tc))
                v.drain()
                if et == 0:
                    v.wait_ge(s_const, W_CONST)
                vinc(v.tensor_tensor_scan(
                    out=wslot(et - 1), data0=wslot(et), data1=wslot(et),
                    initial=0.0, op0=OP.add, op1=OP.bypass), ("scan", et))
                v.drain()
                v.tensor_copy(s16[:, et:et + 1], wslot(et - 1)[:, T - 1:T])
                if et == 7:
                    vinc(v.drain(), ("s16a",))
                elif et == 15:
                    vinc(v.drain(), ("s16b",))
                else:
                    v.drain()
                while pi_done[0] < PIECE_CUM[et]:
                    sel_piece(pi_done[0])
                    pi_done[0] += 1
                if et == 14:
                    v.wait_ge(s_ccsa, 16 * 8)
                    for r in range(NC):
                        v.tensor_scalar(out=ctmp[:, r * 8:(r + 1) * 8],
                                        in0=sg[:, r * 16:r * 16 + 8],
                                        scalar1=maskb[:, r:r + 1], scalar2=None,
                                        op0=OP.mult)
                    v.drain()
                    for r in range(4):
                        v.tensor_tensor(out=ctmp[:, r * 8:(r + 1) * 8],
                                        in0=ctmp[:, r * 8:(r + 1) * 8],
                                        in1=ctmp[:, (r + 4) * 8:(r + 5) * 8],
                                        op=OP.add)
                    v.drain()
                    for r in range(2):
                        v.tensor_tensor(out=ctmp[:, r * 8:(r + 1) * 8],
                                        in0=ctmp[:, r * 8:(r + 1) * 8],
                                        in1=ctmp[:, (r + 2) * 8:(r + 3) * 8],
                                        op=OP.add)
                    v.drain()
                    v.tensor_tensor(out=ctmp[:, 64:72], in0=ctmp[:, 0:8],
                                    in1=ctmp[:, 8:16], op=OP.add)
                    vinc(v.drain(), ("carrya",))
            def carry_col(et):
                return ctmp[:, 64 + et:65 + et] if et < 8 else ctmp[:, 72 + et - 8:73 + et - 8]
            for et in range(16):
                if et == 8:
                    v.wait_ge(s_ccsb, 16 * 8)
                    for r in range(NC):
                        v.tensor_scalar(out=ctmp[:, r * 8:(r + 1) * 8],
                                        in0=sg[:, r * 16 + 8:r * 16 + 16],
                                        scalar1=maskb[:, r:r + 1], scalar2=None,
                                        op0=OP.mult)
                    v.drain()
                    for r in range(4):
                        v.tensor_tensor(out=ctmp[:, r * 8:(r + 1) * 8],
                                        in0=ctmp[:, r * 8:(r + 1) * 8],
                                        in1=ctmp[:, (r + 4) * 8:(r + 5) * 8],
                                        op=OP.add)
                    v.drain()
                    for r in range(2):
                        v.tensor_tensor(out=ctmp[:, r * 8:(r + 1) * 8],
                                        in0=ctmp[:, r * 8:(r + 1) * 8],
                                        in1=ctmp[:, (r + 2) * 8:(r + 3) * 8],
                                        op=OP.add)
                    v.drain()
                    v.tensor_tensor(out=ctmp[:, 72:80], in0=ctmp[:, 0:8],
                                    in1=ctmp[:, 8:16], op=OP.add)
                    vinc(v.drain(), ("carryb",))
                for tc in range(4):
                    v.wait_ge(s_mm, MM_Q(et, tc))
                    qb = banks[tc + 4 * (et % 2)]
                    vinc(v.scalar_tensor_tensor(
                        out=wslot(vslot(et), tc), in0=wslot(et - 1, tc),
                        scalar=carry_col(et), in1=qb[:, :],
                        op0=OP.add, op1=OP.mult), ("vhat", et, tc))
                v.drain()
                if 2 <= et <= 11 and pi_done[0] < NPIECE:
                    sel_piece(pi_done[0])
                    pi_done[0] += 1
            for ft in range(16):
                for tc in range(4):
                    v.wait_ge(s_mm, MM_WO(ft, tc))
                    v.wait_ge(s_xr, W_XRL(ft))
                    if ft >= 1:
                        v.wait_ge(s_sty, W_YA(ft - 1))
                    b = banks[tc + 4 * (ft % 2)]
                    vinc(v.tensor_tensor(
                        out=ystgf[:, tc * 512:(tc + 1) * 512],
                        in0=b[:, :], in1=xrl_wo(ft)[:, tc * 512:(tc + 1) * 512],
                        op=OP.add), ("woev", ft, tc))
                v.drain()
                if ft < 6 and pi_done[0] < NPIECE:
                    sel_piece(pi_done[0])
                    pi_done[0] += 1
            for s, sw in enumerate(SWEEPS):
                for i, ot in enumerate(sw):
                    for st in (1, 2):
                        base = 0 if st == 1 else 4
                        v.wait_ge(s_ac, AC[("silu", ot, st)])
                        v.wait_ge(s_mm, MM_FFN[("u%d" % st, ot)])
                        for k in range(2):
                            tc = (st - 1) * 2 + k
                            inst = v.tensor_tensor(
                                out=wslot(i, tc),
                                in0=sgst[:, (st - 1) * 1024 + k * 512:
                                         (st - 1) * 1024 + (k + 1) * 512],
                                in1=banks[base + 2 + k][:, :], op=OP.mult)
                            if k == 1:
                                vinc(inst, ("hm", ot, st))
                        v.drain()
                for ft in range(16):
                    for tc in range(4):
                        v.wait_ge(s_mm, MM_FFN[("d", s, ft, tc)])
                        v.wait_ge(s_xr, W_YRL(s, ft))
                        if ft >= 1:
                            v.wait_ge(s_sty, W_YD(s, ft - 1))
                        elif s > 0:
                            v.wait_ge(s_sty, W_YD(s - 1, 15))
                        else:
                            v.wait_ge(s_sty, W_YA(15))
                        b = banks[tc + 4 * (ft % 2)]
                        vinc(v.tensor_tensor(
                            out=ystgf[:, tc * 512:(tc + 1) * 512],
                            in0=b[:, :],
                            in1=xrl_ffn(ft)[:, tc * 512:(tc + 1) * 512],
                            op=OP.add), ("dev", s, ft, tc))
                    v.drain()

        @block.scalar
        def _(sc):
            cnt = [0]
            def ainc(inst, key):
                cnt[0] += 1
                assert AC[key] == cnt[0], (key, AC[key], cnt[0])
                inst.then_inc(s_ac, 1)
            sc.wait_ge(s_const, 16)
            sc.dma_start(out=fl_in[:, :], in_=lutb[0:1, 0:16]).then_inc(s_flst, 16)
            for h in range(16):
                sc.wait_ge(s_xf[h % 4], W_XF(h))
                if h >= 4:
                    sc.wait_ge(s_mm, MM_STAT(h - 4))
                ainc(sc.activation(sqb[h % 4][:, :], xftv[h % 4][:, :], AF.Square),
                     ("sq", h))
                sc.drain()
            sc.wait_ge(s_ve, VE[("smath",)])
            ainc(sc.activation(rstdr[:, :], varr[:, :], AF.Sqrt), ("rstd",))
            def idx_load(j):
                aa, bb = CHUNKS[j - 1]
                if j > 2:
                    sc.wait_ge(s_ve, VE[("selchunk", j - 2)])
                else:
                    sc.wait_ge(s_ve, VE[("xn", 15)])
                sc.dma_start(out=selidx[(j - 1) % 2][:, 0:bb - aa],
                             in_=idx_h[:, aa:bb]).then_inc(s_idx[(j - 1) % 2], 16)
            idx_load(1)
            idx_load(2)
            pi_done = 0
            for et in range(16):
                if et == 8:
                    sc.wait_ge(s_ve, VE[("s16a",)])
                    sc.dma_start(out=cs_in_a[:, :],
                                 in_=s16[:, 0:8]).then_inc(s_sts, 16)
                if et == 12:
                    sc.wait_ge(s_cc, CC_DONE["ca"])
                    for r in range(NC):
                        sc.dma_start(out=sg[:, r * 16:r * 16 + 8],
                                     in_=cs_out_a[r, :, :]).then_inc(s_ccsa, 16)
                for tc in range(4):
                    sc.wait_ge(s_mm, MM_K(et, tc))
                    if et >= 2:
                        sc.wait_ge(s_ve, VE[("u", et - 2, 3)])
                    ainc(sc.copy(kdsb[et % 2][:, tc * 512:(tc + 1) * 512],
                                 banks[tc][:, :]), ("kev", et, tc))
                while pi_done < PIECE_CUM[et]:
                    j = PIECES[pi_done][0]
                    if CHUNK_CLOSE_PIECE[j] == pi_done:
                        a, b = CHUNKS[j - 1]
                        sc.wait_ge(s_ve, VE[("selchunk", j)])
                        sc.dma_start(out=cin[j - 1][:, :],
                                     in_=selw[:, 0:b - a]).then_inc(s_stcc, 16)
                        if j + 2 <= 7:
                            idx_load(j + 2)
                    pi_done += 1
            sc.wait_ge(s_ve, VE[("s16b",)])
            sc.dma_start(out=cs_in_b[:, :], in_=s16[:, 8:16]).then_inc(s_sts, 16)
            sc.wait_ge(s_ve, VE[("vhat", 0, 3)])
            sc.dma_start(out=xrlA[:, :], in_=xft_h[0, :, :]).then_inc(s_xr, 16)
            sc.wait_ge(s_cc, CC_DONE["cb"])
            for r in range(NC):
                sc.dma_start(out=sg[:, r * 16 + 8:r * 16 + 16],
                             in_=cs_out_b[r, :, :]).then_inc(s_ccsb, 16)
            idx_load(8)
            sc.wait_ge(s_ve, VE[("selchunk", 6)])
            sc.dma_start(out=cin[5][:, :],
                         in_=selw[:, 0:4096]).then_inc(s_stcc, 16)
            sc.wait_ge(s_ve, VE[("selchunk", 7)])
            sc.dma_start(out=cin[6][:, :],
                         in_=selw[:, 0:5504]).then_inc(s_stcc, 16)
            sc.wait_ge(s_ve, VE[("vhat", 15, 3)])
            sc.dma_start(out=xrlB[:, :], in_=xft_h[1, :, :]).then_inc(s_xr, 16)
            for ft in range(16):
                sc.wait_ge(s_ve, VE[("woev", ft, 3)])
                sc.dma_start(out=y_h[ft, :, :], in_=ystgf[:, :]).then_inc(s_sty, 16)
                if ft + 2 <= 15:
                    sc.dma_start(out=xrl_wo(ft + 2)[:, :],
                                 in_=xft_h[ft + 2, :, :]).then_inc(s_xr, 16)
                if ft == 7:
                    sc.wait_ge(s_ve, VE[("selchunk", 8)])
                    sc.dma_start(out=cin[7][:, :],
                                 in_=selw[:, 0:5504]).then_inc(s_stcc, 16)
            for s, sw in enumerate(SWEEPS):
                for i, ot in enumerate(sw):
                    for st in (1, 2):
                        base = 0 if st == 1 else 4
                        sc.wait_ge(s_mm, MM_FFN[("g%d" % st, ot)])
                        if ot > 0:
                            sc.wait_ge(s_ve, VE[("hm", ot - 1, st)])
                        for k in range(2):
                            inst = sc.activation(
                                sgst[:, (st - 1) * 1024 + k * 512:
                                     (st - 1) * 1024 + (k + 1) * 512],
                                banks[base + k][:, :], AF.Silu)
                            if k == 1:
                                ainc(inst, ("silu", ot, st))
                        sc.drain()
                for ft in range(16):
                    sc.wait_ge(s_ve, VE[("dev", s, ft, 3)])
                    sc.dma_start(out=y_h[ft, :, :],
                                 in_=ystgf[:, :]).then_inc(s_sty, 16)
            sc.wait_ge(s_sty, STY_FINAL)

        @block.gpsimd
        def _(g):
            for et in range(2):
                g.dma_start(out=wkb_d[et, :, :],
                            in_=wk_h[et, :, :]).then_inc(s_ck, 16)
                g.dma_start(out=wvb_d[et, :, :],
                            in_=wv_h[et, :, :]).then_inc(s_cv, 16)
            for h in range(16):
                if h == 0:
                    g.wait_ge(s_ve, VE[("bevac",)])
                if h >= 2:
                    g.wait_ge(s_ve, VE[("xn", h - 2)])
                g.wait_ge(s_xf[h % 2], W_XF2(h))
                g.tensor_tensor(out=t1slot(h)[:, :],
                                in0=xftv[h % 2][:, :], in1=mbv[:, :],
                                op=OP.add).then_inc(s_gp, 1)
            for et in range(2, 16):
                g.dma_start(out=wkb_d[et, :, :],
                            in_=wk_h[et, :, :]).then_inc(s_ck, 16)
                g.dma_start(out=wvb_d[et, :, :],
                            in_=wv_h[et, :, :]).then_inc(s_cv, 16)
            for et in range(16):
                g.dma_start(out=wqb_d[et, :, :],
                            in_=wq_h[et, :, :]).then_inc(s_cq, 16)
            for et in range(16):
                g.dma_start(out=wob_d[et, :, :],
                            in_=wo_h[et, :, :]).then_inc(s_co, 16)
            def coll(inbuf, outbuf):
                g.collective_compute(
                    "AllGather", OP.bypass,
                    replica_groups=[list(range(NC))],
                    ins=[inbuf], outs=[outbuf]).then_inc(s_cc, 1)
            g.wait_ge(s_stcc, W_STCC(1))
            coll(cin[0][:, :], cout[0][:, :, :])
            g.wait_ge(s_stcc, W_STCC(2))
            coll(cin[1][:, :], cout[1][:, :, :])
            g.wait_ge(s_stcc, W_STCC(3))
            coll(cin[2][:, :], cout[2][:, :, :])
            g.wait_ge(s_sts, 16)
            coll(cs_in_a[:, :], cs_out_a[:, :, :])
            g.wait_ge(s_flst, 16)
            coll(fl_in[:, :], fl_out[0][:, :, :])
            g.wait_ge(s_stcc, W_STCC(4))
            coll(cin[3][:, :], cout[3][:, :, :])
            g.wait_ge(s_sts, 32)
            coll(cs_in_b[:, :], cs_out_b[:, :, :])
            coll(fl_in[:, :], fl_out[1][:, :, :])
            g.wait_ge(s_stcc, W_STCC(5))
            coll(cin[4][:, :], cout[4][:, :, :])
            g.wait_ge(s_stcc, W_STCC(6))
            coll(cin[5][:, :], cout[5][:, :, :])
            g.wait_ge(s_stcc, W_STCC(7))
            coll(cin[6][:, :], cout[6][:, :, :])
            g.wait_ge(s_stcc, W_STCC(8))
            coll(cin[7][:, :], cout[7][:, :, :])
            coll(fl_in[:, :], fl_out[2][:, :, :])
            for s in range(3):
                for ft in range(16):
                    if s == 0 and ft == 0:
                        g.wait_ge(s_ve, VE[("woev", 14, 3)])
                        g.wait_ge(s_sty, W_YA(0))
                    elif s == 0 and ft == 1:
                        g.wait_ge(s_mm, MM_WO_END)
                        g.wait_ge(s_sty, W_YA(1))
                    else:
                        if ft >= 2:
                            g.wait_ge(s_ve, VE[("dev", s, ft - 2, 3)])
                        else:
                            g.wait_ge(s_ve, VE[("dev", s - 1, 14 + ft, 3)])
                        g.wait_ge(s_sty,
                                  W_YA(ft) if s == 0 else W_YD(s - 1, ft))
                    g.dma_start(out=xrl_ffn(ft)[:, :],
                                in_=y_h[ft, :, :]).then_inc(s_xr, 16)

        @block.sync
        def _(sy):
            sy.dma_start(out=lutb[:, :], in_=lutb_h[:, :]).then_inc(s_const, 16)
            sy.dma_start(out=maskb[:, :], in_=mask_h[:, :]).then_inc(s_const, 16)
            for h in range(16):
                if h >= 4:
                    sy.wait_ge(s_ve, VE[("cast", h - 4)])
                    sy.wait_ge(s_ac, AC[("sq", h - 4)])
                sy.dma_start(out=xftv[h % 4][:, :],
                             in_=xft_h[h, :, :]).then_inc(s_xf[h % 4], 16)
            for et in range(2):
                sy.wait_ge(s_ck, 16 * (et + 1))
                sy.dma_start(out=wkvb[:, (et % 2) * D:(et % 2 + 1) * D],
                             in_=wkb_d[et, :, :]).then_inc(s_k[et % 2], 16)
                sy.wait_ge(s_cv, 16 * (et + 1))
                sy.dma_start(out=wkvb[:, (2 + et % 2) * D:(3 + et % 2) * D],
                             in_=wvb_d[et, :, :]).then_inc(s_v[et % 2], 16)
            for h in range(16):
                if h < 2:
                    sy.wait_ge(s_ve, VE[("cast", 12 + h)])
                    sy.wait_ge(s_ac, AC[("sq", 12 + h)])
                else:
                    sy.wait_ge(s_gp, h - 1)
                sy.dma_start(out=xftv[h % 2][:, :],
                             in_=xft_h[h, :, :]).then_inc(s_xf[h % 2], 16)
            for et in range(2, 16):
                sy.wait_ge(s_ck, 16 * (et + 1))
                sy.wait_ge(s_mm, MM_K(et - 2, 3))
                sy.dma_start(out=wkvb[:, (et % 2) * D:(et % 2 + 1) * D],
                             in_=wkb_d[et, :, :]).then_inc(s_k[et % 2], 16)
                sy.wait_ge(s_cv, 16 * (et + 1))
                sy.wait_ge(s_mm, MM_V(et - 2, 3))
                sy.dma_start(out=wkvb[:, (2 + et % 2) * D:(3 + et % 2) * D],
                             in_=wvb_d[et, :, :]).then_inc(s_v[et % 2], 16)
            for et in range(16):
                sy.wait_ge(s_cq, 16 * (et + 1))
                if et < 2:
                    sy.wait_ge(s_mm, MM_K(14 + et, 3))
                else:
                    sy.wait_ge(s_mm, MM_Q(et - 2, 3))
                sy.dma_start(out=wkvb[:, (et % 2) * D:(et % 2 + 1) * D],
                             in_=wqb_d[et, :, :]).then_inc(s_k[et % 2], 16)
            for ft in range(16):
                sy.wait_ge(s_co, 16 * (ft + 1))
                if ft < 2:
                    sy.wait_ge(s_mm, MM_V(14 + ft, 3))
                else:
                    sy.wait_ge(s_mm, MM_WO(ft - 2, 3))
                sy.dma_start(out=wkvb[:, (2 + ft % 2) * D:(3 + ft % 2) * D],
                             in_=wob_d[ft, :, :]).then_inc(s_v[ft % 2], 16)
            for s, sw in enumerate(SWEEPS):
                for ot in sw:
                    rg, colg = _gate_loc(ot)
                    ru, colu = _up_loc(ot)
                    jg, ju = _chunk_of_col(colg), _chunk_of_col(colu)
                    ag = CHUNKS[jg - 1][0]
                    au = CHUNKS[ju - 1][0]
                    sy.wait_ge(s_cc, CC_DONE[jg])
                    if ot < 2:
                        sy.wait_ge(s_stcc, W_STCC(8))
                    else:
                        sy.wait_ge(s_mm, MM_FFN[("g2", ot - 2)])
                    sy.dma_start(out=gbuf[ot % 2][:, :],
                                 in_=cout[jg - 1][rg, :, colg - ag:colg - ag + 2048]
                                 ).then_inc(s_g[ot % 2], 16)
                    sy.wait_ge(s_cc, CC_DONE[ju])
                    if ot < 2:
                        sy.wait_ge(s_stcc, W_STCC(8))
                    else:
                        sy.wait_ge(s_mm, MM_FFN[("u2", ot - 2)])
                    sy.dma_start(out=ubuf[ot % 2][:, :],
                                 in_=cout[ju - 1][ru, :, colu - au:colu - au + 2048]
                                 ).then_inc(s_u[ot % 2], 16)
                nsw = len(sw)
                it0 = sw[0]
                for ft in range(16):
                    rd, cold = _down_loc(ft)
                    jd = _chunk_of_col(cold)
                    ad = CHUNKS[jd - 1][0]
                    sy.wait_ge(s_cc, CC_DONE[jd])
                    if s == 0 and ft < 2:
                        sy.wait_ge(s_ve, VE[("u", 15, 3)])
                        sy.wait_ge(s_stcc, W_STCC(8))
                    if ft >= 2:
                        sy.wait_ge(s_mm, MM_FFN[("d", s, ft - 2, 3)])
                    elif s > 0:
                        sy.wait_ge(s_mm, MM_FFN[("d", s - 1, 14 + ft, 3)])
                    off = cold - ad + it0 * 128
                    sy.dma_start(out=dbuf[ft % 2][:, 0:nsw * 128],
                                 in_=cout[jd - 1][rd, :, off:off + nsw * 128]
                                 ).then_inc(s_d[ft % 2], 16)

        @block.tensor
        def _(te):
            cnt = [0]
            def minc(inst, val):
                cnt[0] += 1
                assert val == cnt[0], (val, cnt[0])
                inst.then_inc(s_mm, 1)
            te.wait_ge(s_ve, VE[("init",)])
            for h in range(16):
                te.wait_ge(s_ve, VE[("cast", h)])
                for tc in range(4):
                    te.matmul(banks[tc][0:1, :], onesb[:, 0:1],
                              xn[:, h * T + tc * 512:h * T + (tc + 1) * 512],
                              start=(h == 0), stop=(h == 15))
                te.wait_ge(s_ac, AC[("sq", h)])
                last = None
                for tc in range(4):
                    last = te.matmul(banks[4 + tc][0:1, :], onesb[:, 0:1],
                                     sqb[h % 4][:, tc * 512:(tc + 1) * 512],
                                     start=(h == 0), stop=(h == 15))
                minc(last, MM_STAT(h))
            te.wait_ge(s_ve, VE[("scast",)])
            for tc in range(4):
                minc(te.matmul(banks[tc][:, :], onesb[0:1, 0:128],
                               mstat_bf[:, tc * 512:(tc + 1) * 512],
                               start=True, stop=True), MM_BCM(tc))
            for tc in range(4):
                minc(te.matmul(banks[4 + tc][:, :], onesb[0:1, 0:128],
                               rstat_bf[:, tc * 512:(tc + 1) * 512],
                               start=True, stop=True), MM_BCM(4 + tc))
            te.wait_ge(s_ve, VE[("bevac",)])
            te.wait_ge(s_k[0], W_K(0))
            te.wait_ge(s_v[0], W_V(0))
            lastk, lastv = {}, {}
            for dd in range(16):
                te.wait_ge(s_ve, VE[("xn", dd)])
                for tc in range(4):
                    lastk[tc] = te.matmul(
                        banks[tc][:, :],
                        wkvb[:, dd * 128:(dd + 1) * 128],
                        xn[:, dd * T + tc * 512:dd * T + (tc + 1) * 512],
                        start=(dd == 0), stop=(dd == 15))
                for tc in range(4):
                    lastv[tc] = te.matmul(
                        banks[4 + tc][:, :],
                        wkvb[:, 2 * D + dd * 128:2 * D + (dd + 1) * 128],
                        xn[:, dd * T + tc * 512:dd * T + (tc + 1) * 512],
                        start=(dd == 0), stop=(dd == 15))
            for tc in range(4):
                minc(lastk[tc], MM_K(0, tc))
            for tc in range(4):
                minc(lastv[tc], MM_V(0, tc))
            for et in range(1, 16):
                te.wait_ge(s_k[et % 2], W_K(et))
                for tc in range(4):
                    te.wait_ge(s_ac, AC[("kev", et - 1, tc)])
                last = {}
                for dd in range(16):
                    for tc in range(4):
                        last[tc] = te.matmul(
                            banks[tc][:, :],
                            wkvb[:, (et % 2) * D + dd * 128:(et % 2) * D + (dd + 1) * 128],
                            xn[:, dd * T + tc * 512:dd * T + (tc + 1) * 512],
                            start=(dd == 0), stop=(dd == 15))
                for tc in range(4):
                    minc(last[tc], MM_K(et, tc))
                te.wait_ge(s_v[et % 2], W_V(et))
                for tc in range(4):
                    te.wait_ge(s_ve, VE[("u", et - 1, tc)])
                for dd in range(16):
                    for tc in range(4):
                        last[tc] = te.matmul(
                            banks[4 + tc][:, :],
                            wkvb[:, (2 + et % 2) * D + dd * 128:(2 + et % 2) * D + (dd + 1) * 128],
                            xn[:, dd * T + tc * 512:dd * T + (tc + 1) * 512],
                            start=(dd == 0), stop=(dd == 15))
                for tc in range(4):
                    minc(last[tc], MM_V(et, tc))
            for et in range(16):
                te.wait_ge(s_k[et % 2], W_Q(et))
                if et == 0:
                    for tc in range(4):
                        te.wait_ge(s_ac, AC[("kev", 15, tc)])
                elif et == 1:
                    for tc in range(4):
                        te.wait_ge(s_ve, VE[("u", 15, tc)])
                else:
                    for tc in range(4):
                        te.wait_ge(s_ve, VE[("vhat", et - 2, tc)])
                last = {}
                for dd in range(16):
                    for tc in range(4):
                        last[tc] = te.matmul(
                            banks[tc + 4 * (et % 2)][:, :],
                            wkvb[:, (et % 2) * D + dd * 128:(et % 2) * D + (dd + 1) * 128],
                            xn[:, dd * T + tc * 512:dd * T + (tc + 1) * 512],
                            start=(dd == 0), stop=(dd == 15))
                for tc in range(4):
                    minc(last[tc], MM_Q(et, tc))
            te.wait_ge(s_ve, VE[("vhat", 15, 3)])
            for ft in range(16):
                te.wait_ge(s_v[ft % 2], W_WOC(ft))
                if ft < 2:
                    for tc in range(4):
                        te.wait_ge(s_ve, VE[("vhat", 14 + ft, tc)])
                else:
                    for tc in range(4):
                        te.wait_ge(s_ve, VE[("woev", ft - 2, tc)])
                last = {}
                for ee in range(16):
                    for tc in range(4):
                        last[tc] = te.matmul(
                            banks[tc + 4 * (ft % 2)][:, :],
                            wkvb[:, (2 + ft % 2) * D + ee * 128:(2 + ft % 2) * D + (ee + 1) * 128],
                            wslot(vslot(ee), tc),
                            start=(ee == 0), stop=(ee == 15))
                for tc in range(4):
                    minc(last[tc], MM_WO(ft, tc))
            for s, sw in enumerate(SWEEPS):
                for i, ot in enumerate(sw):
                    te.wait_ge(s_g[ot % 2], W_G(ot))
                    te.wait_ge(s_u[ot % 2], W_U(ot))
                    if s == 0 and i == 0:
                        te.wait_ge(s_ve, VE[("woev", 14, 3)])
                    elif i == 0:
                        te.wait_ge(s_ve, VE[("dev", s - 1, 14, 3)])
                    else:
                        te.wait_ge(s_ac, AC[("silu", ot - 1, 1)])
                        te.wait_ge(s_ve, VE[("hm", ot - 1, 1)])
                    lastg, lastu = {}, {}
                    for dd in range(16):
                        for k in range(2):
                            lastg[k] = te.matmul(
                                banks[k][:, :],
                                gbuf[ot % 2][:, dd * 128:(dd + 1) * 128],
                                xn[:, dd * T + k * 512:dd * T + (k + 1) * 512],
                                start=(dd == 0), stop=(dd == 15))
                        for k in range(2):
                            lastu[k] = te.matmul(
                                banks[2 + k][:, :],
                                ubuf[ot % 2][:, dd * 128:(dd + 1) * 128],
                                xn[:, dd * T + k * 512:dd * T + (k + 1) * 512],
                                start=(dd == 0), stop=(dd == 15))
                    minc(lastg[1], MM_FFN[("g1", ot)])
                    minc(lastu[1], MM_FFN[("u1", ot)])
                    if i > 0:
                        te.wait_ge(s_ac, AC[("silu", ot - 1, 2)])
                        te.wait_ge(s_ve, VE[("hm", ot - 1, 2)])
                    elif s == 0:
                        te.wait_ge(s_ve, VE[("woev", 15, 3)])
                    else:
                        te.wait_ge(s_ve, VE[("dev", s - 1, 15, 3)])
                    for dd in range(16):
                        for k in range(2):
                            lastg[k] = te.matmul(
                                banks[4 + k][:, :],
                                gbuf[ot % 2][:, dd * 128:(dd + 1) * 128],
                                xn[:, dd * T + (2 + k) * 512:dd * T + (3 + k) * 512],
                                start=(dd == 0), stop=(dd == 15))
                        for k in range(2):
                            lastu[k] = te.matmul(
                                banks[6 + k][:, :],
                                ubuf[ot % 2][:, dd * 128:(dd + 1) * 128],
                                xn[:, dd * T + (2 + k) * 512:dd * T + (3 + k) * 512],
                                start=(dd == 0), stop=(dd == 15))
                    minc(lastg[1], MM_FFN[("g2", ot)])
                    minc(lastu[1], MM_FFN[("u2", ot)])
                nsw = len(sw)
                te.wait_ge(s_ve, VE[("hm", sw[-1], 2)])
                for ft in range(16):
                    te.wait_ge(s_d[ft % 2], W_D(s, ft))
                    if ft == 0:
                        te.wait_ge(s_ac, AC[("silu", sw[-1], 1)])
                    elif ft == 1:
                        te.wait_ge(s_ac, AC[("silu", sw[-1], 2)])
                    else:
                        for tc in range(4):
                            te.wait_ge(s_ve, VE[("dev", s, ft - 2, tc)])
                    last = {}
                    for iti in range(nsw):
                        for tc in range(4):
                            last[tc] = te.matmul(
                                banks[tc + 4 * (ft % 2)][:, :],
                                dbuf[ft % 2][:, iti * 128:(iti + 1) * 128],
                                wslot(iti, tc),
                                start=(iti == 0), stop=(iti == nsw - 1))
                    for tc in range(4):
                        minc(last[tc], MM_FFN[("d", s, ft, tc)])

    return nc


def _blk(W):
    return np.ascontiguousarray(
        W.reshape(16, 128, 16, 128).transpose(0, 3, 2, 1).reshape(16, 128, D))


def kernel(L0_attn_x, L0_attn_Wq, L0_attn_Wk, L0_attn_Wv, L0_attn_Wo,
           L0_ffn_lut, L0_ffn_gate_idx, L0_ffn_up_idx, L0_ffn_down_idx):
    x = np.asarray(L0_attn_x, dtype=np.float32)
    Wq = np.asarray(L0_attn_Wq, dtype=np.float32)
    Wk = np.asarray(L0_attn_Wk, dtype=np.float32)
    Wv = np.asarray(L0_attn_Wv, dtype=np.float32)
    Wo = np.asarray(L0_attn_Wo, dtype=np.float32)
    lut = np.asarray(L0_ffn_lut, dtype=np.float32)
    gidx = np.asarray(L0_ffn_gate_idx)
    uidx = np.asarray(L0_ffn_up_idx)
    didx = np.asarray(L0_ffn_down_idx)

    wqb, wkb, wvb, wob = _blk(Wq), _blk(Wk), _blk(Wv), _blk(Wo)
    lutb = np.ascontiguousarray(np.broadcast_to(lut[None, :], (128, 16)))

    g_can = gidx.reshape(43, 128, 16, 128).transpose(0, 3, 2, 1).reshape(43, 128, 2048)
    u_can = uidx.reshape(43, 128, 16, 128).transpose(0, 3, 2, 1).reshape(43, 128, 2048)
    d_can = didx.reshape(16, 128, 43, 128).transpose(0, 3, 2, 1).reshape(16, 128, 5504)

    in_maps = []
    for c in range(NC):
        bb, jj = c // 4, c % 4
        xs = x[bb, jj * T:(jj + 1) * T, :]
        xft = np.ascontiguousarray(xs.T.reshape(16, 128, T))
        reg = np.zeros((128, REG_COLS), np.float32)
        a0, b0 = G_OWN[c]
        for s, ot in enumerate(range(a0, b0)):
            reg[:, s * 4096:s * 4096 + 2048] = g_can[ot]
            reg[:, s * 4096 + 2048:(s + 1) * 4096] = u_can[ot]
        for s, ft in enumerate((2 * c, 2 * c + 1)):
            reg[:, REG_D0 + s * 5504:REG_D0 + (s + 1) * 5504] = d_can[ft]
        regb = reg.astype(ml_dtypes.bfloat16)
        mask = np.zeros((128, 8), np.float32)
        for r in range(NC):
            if r // 4 == bb and (r % 4) < jj:
                mask[:, r] = 1.0
        in_maps.append({
            "xft_h": xft,
            "wq_h": wqb, "wk_h": wkb, "wv_h": wvb, "wo_h": wob,
            "idx_h": regb, "lutb_h": lutb, "mask_h": mask,
        })

    nc = _build_nc()
    res = run_bass_kernel_spmd(nc, in_maps, core_ids=list(range(NC)))
    out = np.empty((2, 8192, D), np.float32)
    for c in range(NC):
        bb, jj = c // 4, c % 4
        yf = np.asarray(res.results[c]["y_h"]).reshape(D, T)
        out[bb, jj * T:(jj + 1) * T, :] = yf.T
    return out


# version 16 (final)
# speedup vs baseline: 1.0280x; 1.0280x over previous
import numpy as np
import ml_dtypes
from contextlib import ExitStack

import concourse.bass as bass
import concourse.mybir as mybir
from concourse.bass_utils import run_bass_kernel_spmd

F32 = mybir.dt.float32
BF16 = mybir.dt.bfloat16
OP = mybir.AluOpType
AF = mybir.ActivationFunctionType

NC = 8
T = 2048
D = 2048
NH = 16
DENOM = float(np.sqrt(2048.0) / 8192.0)
EPS = 1e-5

G_OWN = [(0, 6), (6, 12), (12, 17), (17, 22), (22, 28), (28, 33), (33, 38), (38, 43)]
GPAD = 6
REG_G0 = 0
REG_U0 = GPAD * 2048
REG_D0 = 2 * GPAD * 2048
REG_COLS = REG_D0 + 2 * 5504
CHUNKS = [(0, 4096), (4096, 8192), (8192, 12288),
          (12288, 16384), (16384, 20480), (20480, 24576),
          (24576, 30080), (30080, 35584)]
NCHUNK = len(CHUNKS)
CC_DONE = {1: 2, 2: 3, 3: 4, 4: 7, 5: 10, 6: 11, 7: 12, 8: 13,
           "ca": 5, "cb": 8}

SWEEPS = [list(range(0, 15)), list(range(15, 29)), list(range(29, 43))]

def _gate_loc(ot):
    for c, (a, b) in enumerate(G_OWN):
        if a <= ot < b:
            return c, (ot - a) * 4096
    raise AssertionError(ot)

def _up_loc(ot):
    c, col = _gate_loc(ot)
    return c, col + 2048

def _down_loc(ft):
    return ft // 2, REG_D0 + (ft % 2) * 5504

def _chunk_of_col(col):
    for j, (a, b) in enumerate(CHUNKS):
        if a <= col < b:
            return j + 1
    raise AssertionError(col)

def _mk_pieces():
    out = []
    for j, (a, b) in enumerate(CHUNKS):
        col = a
        while col < b:
            w = min(1024, b - col)
            out.append((j + 1, col, w))
            col += w
    return out

PIECES = _mk_pieces()
NPIECE = len(PIECES)
PIECE_CUM = [min(20, int(np.ceil((et + 1) * 1.25))) for et in range(16)]
CHUNK_CLOSE_PIECE = {}
for _pi, (_j, _c, _w) in enumerate(PIECES):
    if _pi + 1 == NPIECE or PIECES[_pi + 1][0] != _j:
        CHUNK_CLOSE_PIECE[_j] = _pi

def W_XF(h):       return 16 * (h // 4 + 1)
def W_XF2(h):      return 16 * (4 + h // 2 + 1)
def W_K(et):       return 16 * (et // 2 + 1)
def W_Q(et):       return 16 * (8 + et // 2 + 1)
def W_V(et):       return 16 * (et // 2 + 1)
def W_WOC(ft):     return 16 * (8 + ft // 2 + 1)
def W_XRL(ft):     return 16 * (ft + 1)
def W_YRL(s, ft):  return 16 * (16 + 16 * s + ft + 1)
def W_G(ot):       return 16 * (ot // 2 + 1)
def W_U(ot):       return 16 * (ot // 2 + 1)
def W_D(s, ft):    return 16 * ((16 * s + ft) // 2 + 1)
def W_IDX(j):      return 16 * ((j - 1) // 2 + 1)
W_CONST = 16 * 2
def W_STCC(j):     return 16 * j
def W_YA(ft):      return 16 * (ft + 1)
def W_YD(s, ft):   return 16 * (16 + 16 * s + ft + 1)
STY_FINAL = 16 * 64

def MM_STAT(h):
    return h + 1

def MM_BCM(i):
    return 17 + i

MM_OFF = 24

def MM_K(et, tc):
    return MM_OFF + 8 * et + tc + 1

def MM_V(et, tc):
    return MM_OFF + 8 * et + 4 + tc + 1

MM_A_END = MM_OFF + 128

def MM_Q(et, tc):
    return MM_A_END + 4 * et + tc + 1

MM_Q_END = MM_A_END + 64

def MM_WO(ft, tc):
    return MM_Q_END + 4 * ft + tc + 1

MM_WO_END = MM_Q_END + 64

def _mm_ffn_plan():
    val = MM_WO_END
    plan = {}
    for s, sw in enumerate(SWEEPS):
        for ot in sw:
            for tag in ("g1", "u1", "g2", "u2"):
                val += 1
                plan[(tag, ot)] = val
        for ft in range(16):
            for tc in range(4):
                val += 1
                plan[("d", s, ft, tc)] = val
    return plan

MM_FFN = _mm_ffn_plan()

def _ac_plan():
    val = 0
    plan = {}
    def put(k):
        nonlocal val
        val += 1
        plan[k] = val
    for h in range(16):
        put(("sq", h))
    put(("rstd",))
    for et in range(16):
        for tc in range(4):
            put(("kev", et, tc))
    for s, sw in enumerate(SWEEPS):
        for ot in sw:
            put(("silu", ot, 1))
            put(("silu", ot, 2))
    return plan

AC = _ac_plan()

def _ve_plan():
    val = 0
    plan = {}
    def put(k):
        nonlocal val
        val += 1
        plan[k] = val
    put(("init",))
    for h in range(16):
        put(("cast", h))
    put(("smath",))
    put(("scast",))
    put(("bevac",))
    for h in range(16):
        put(("xn", h))
    pi = 0
    for et in range(16):
        for tc in range(4):
            put(("u", et, tc))
        put(("scan", et))
        if et == 7:
            put(("s16a",))
        if et == 15:
            put(("s16b",))
        while pi < PIECE_CUM[et]:
            j = PIECES[pi][0]
            if CHUNK_CLOSE_PIECE[j] == pi:
                put(("selchunk", j))
            pi += 1
    put(("carrya",))
    for et in range(16):
        if et == 8:
            put(("carryb",))
        for tc in range(4):
            put(("vhat", et, tc))
        if 2 <= et <= 13 and pi < NPIECE:
            j = PIECES[pi][0]
            if CHUNK_CLOSE_PIECE[j] == pi:
                put(("selchunk", j))
            pi += 1
    for ft in range(16):
        for tc in range(4):
            put(("woev", ft, tc))
        if ft < 4 and pi < NPIECE:
            j = PIECES[pi][0]
            if CHUNK_CLOSE_PIECE[j] == pi:
                put(("selchunk", j))
            pi += 1
    for s, sw in enumerate(SWEEPS):
        for ot in sw:
            put(("hm", ot, 1))
            put(("hm", ot, 2))
        for ft in range(16):
            for tc in range(4):
                put(("dev", s, ft, tc))
    return plan

VE = _ve_plan()


def _build_nc():
    nc = bass.Bass("TRN2", target_bir_lowering=False, debug=False)

    xft_h = nc.dram_tensor("xft_h", [16, 128, T], F32, kind="ExternalInput").ap()
    wq_h = nc.dram_tensor("wq_h", [16, 128, D], F32, kind="ExternalInput").ap()
    wk_h = nc.dram_tensor("wk_h", [16, 128, D], F32, kind="ExternalInput").ap()
    wv_h = nc.dram_tensor("wv_h", [16, 128, D], F32, kind="ExternalInput").ap()
    wo_h = nc.dram_tensor("wo_h", [16, 128, D], F32, kind="ExternalInput").ap()
    idx_h = nc.dram_tensor("idx_h", [128, REG_COLS], BF16, kind="ExternalInput").ap()
    lutb_h = nc.dram_tensor("lutb_h", [128, 16], F32, kind="ExternalInput").ap()
    mask_h = nc.dram_tensor("mask_h", [128, 8], F32, kind="ExternalInput").ap()
    y_h = nc.dram_tensor("y_h", [16, 128, T], F32, kind="ExternalOutput").ap()

    cin, cout = [], []
    for j in range(1, NCHUNK + 1):
        a, b = CHUNKS[j - 1]
        cin.append(nc.dram_tensor(f"cc_in{j}", [128, b - a], BF16, kind="Internal").ap())
        cout.append(nc.dram_tensor(f"cc_out{j}", [NC, 128, b - a], BF16,
                                   kind="Internal", addr_space="Shared").ap())
    cs_in_a = nc.dram_tensor("cs_in_a", [128, 8], F32, kind="Internal").ap()
    cs_out_a = nc.dram_tensor("cs_out_a", [NC, 128, 8], F32, kind="Internal",
                              addr_space="Shared").ap()
    cs_in_b = nc.dram_tensor("cs_in_b", [128, 8], F32, kind="Internal").ap()
    cs_out_b = nc.dram_tensor("cs_out_b", [NC, 128, 8], F32, kind="Internal",
                              addr_space="Shared").ap()
    fl_in = nc.dram_tensor("fl_in", [1, 16], F32, kind="Internal").ap()
    fl_out = [nc.dram_tensor(f"fl_out{i}", [NC, 1, 16], F32, kind="Internal",
                             addr_space="Shared").ap() for i in range(3)]
    wkb_d = nc.dram_tensor("wkb_d", [16, 128, D], BF16, kind="Internal").ap()
    wvb_d = nc.dram_tensor("wvb_d", [16, 128, D], BF16, kind="Internal").ap()
    wqb_d = nc.dram_tensor("wqb_d", [16, 128, D], BF16, kind="Internal").ap()
    wob_d = nc.dram_tensor("wob_d", [16, 128, D], BF16, kind="Internal").ap()

    es = ExitStack()
    def sbt(name, shape, dtype):
        return es.enter_context(nc.sbuf_tensor(name, shape, dtype))

    xn = sbt("xn", [128, NH * T], BF16)
    work = sbt("work", [128, 16 * T], BF16)
    big = sbt("big", [128, 31872], BF16)
    wkvb = sbt("wkvb", [128, 4 * D], BF16)
    s16 = sbt("s16", [128, 16], F32)
    sg = sbt("sg", [128, NC * 16], F32)
    ctmp = sbt("ctmp", [128, 128], F32)
    maskb = sbt("maskb", [128, 8], F32)
    lutb = sbt("lutb", [128, 16], F32)
    onesb = sbt("onesb", [128, 128], BF16)

    def bigv(c0, cols, dtype=BF16):
        apb = big[:, c0:c0 + cols]
        return apb if dtype == BF16 else apb.bitcast(dtype)

    sqb = [bigv(0, 2048), bigv(2048, 2048),
           bigv(4096, 2048), bigv(6144, 2048)]
    t1B = bigv(4096, 4096, F32)
    mneg = bigv(4096, 4096, F32)[0:1, :]
    xftv = [bigv(8192, 4096, F32), bigv(12288, 4096, F32),
            bigv(16384, 4096, F32), bigv(20480, 4096, F32)]
    mstat_bf = bigv(16384, 2048)[0:1, :]
    t1A = bigv(16384, 4096, F32)
    varr = bigv(24576, 4096, F32)[0:1, :]
    mbv = bigv(20480, 4096, F32)
    rstdr = bigv(20480, 4096, F32)[0:1, :]
    rbv = bigv(24576, 4096, F32)
    rstat_bf = bigv(18432, 2048)[0:1, :]
    selidx = [bigv(0, 5504), bigv(5504, 5504)]
    selw = bigv(11008, 5504)
    seltmp = bigv(16512, 1024)
    selacc = [bigv(17536, 1024), bigv(18560, 1024)]
    sgst = bigv(16512, 2048)
    kdsb = [bigv(19584, 2048), bigv(21632, 2048)]
    spare = bigv(23680, 2048)
    vtile0 = bigv(19584, 2048)
    vtile1 = bigv(21632, 2048)
    xrlA = bigv(23680, 4096, F32)
    xrlB = work[:, 14 * T:16 * T].bitcast(F32)
    xrlC = bigv(19584, 4096, F32)
    ystgf = bigv(27776, 4096, F32)
    gbuf = [bigv(0, 2048), bigv(2048, 2048)]
    ubuf = [bigv(4096, 2048), bigv(6144, 2048)]
    dbuf = [bigv(8192, 1920), bigv(10112, 1920)]

    banks = [es.enter_context(nc.psum_tensor(f"ps{i}", [128, 512], F32))
             for i in range(8)]

    def sem(name):
        return es.enter_context(nc.semaphore(name))
    s_xf = [sem("s_xf0"), sem("s_xf1"), sem("s_xf2"), sem("s_xf3")]
    s_k = [sem("s_k0"), sem("s_k1")]
    s_v = [sem("s_v0"), sem("s_v1")]
    s_xr = sem("s_xr")
    s_g = [sem("s_g0"), sem("s_g1")]
    s_u = [sem("s_u0"), sem("s_u1")]
    s_d = [sem("s_d0"), sem("s_d1")]
    s_idx = [sem("s_idx0"), sem("s_idx1")]
    s_ccsa, s_ccsb, s_const = sem("s_ccsa"), sem("s_ccsb"), sem("s_const")
    s_stcc, s_sts, s_flst = sem("s_stcc"), sem("s_sts"), sem("s_flst")
    s_ck, s_cv, s_cq, s_co = sem("s_ck"), sem("s_cv"), sem("s_cq"), sem("s_co")
    s_sty = sem("s_sty")
    s_cc, s_mm, s_ve, s_ac = sem("s_cc"), sem("s_mm"), sem("s_ve"), sem("s_ac")
    s_gp = sem("s_gp")

    def vslot(et):
        return ("v0" if et == 0 else ("v1" if et == 1 else et - 2))

    def wslot(i, tc=None):
        if i == "v0":
            base, c0 = vtile0, 0
        elif i == "v1":
            base, c0 = vtile1, 0
        elif i < 0:
            base, c0 = spare, 0
        else:
            base, c0 = work, i * T
        if tc is None:
            return base[:, c0:c0 + T]
        return base[:, c0 + tc * 512: c0 + (tc + 1) * 512]

    def t1slot(h):
        return t1A if h % 2 == 0 else t1B

    def xrl_wo(ft):
        return xrlA if ft % 2 == 0 else xrlB

    def xrl_ffn(ft):
        return xrlA if ft % 2 == 0 else xrlC

    with es:
        block = es.enter_context(nc.Block())

        @block.vector
        def _(v):
            cnt = [0]
            def vinc(inst, key):
                cnt[0] += 1
                assert VE[key] == cnt[0], (key, VE[key], cnt[0])
                inst.then_inc(s_ve, 1)
            vinc(v.memset(onesb[:, :], 1.0), ("init",))
            for h in range(16):
                v.wait_ge(s_xf[h % 4], W_XF(h))
                vinc(v.tensor_copy(xn[:, h * T:(h + 1) * T], xftv[h % 4][:, :]),
                     ("cast", h))
            v.wait_ge(s_mm, MM_STAT(15))
            for tc in range(4):
                v.tensor_scalar(out=mneg[:, tc * 512:(tc + 1) * 512],
                                in0=banks[tc][0:1, :],
                                scalar1=-1.0 / D, scalar2=None, op0=OP.mult)
            v.drain()
            v.tensor_tensor(out=rstdr[:, :], in0=mneg[:, :], in1=mneg[:, :],
                            op=OP.mult)
            for tc in range(4):
                v.tensor_scalar(out=varr[:, tc * 512:(tc + 1) * 512],
                                in0=banks[4 + tc][0:1, :],
                                scalar1=1.0 / D, scalar2=EPS,
                                op0=OP.mult, op1=OP.add)
            v.drain()
            vinc(v.tensor_tensor(out=varr[:, :], in0=varr[:, :], in1=rstdr[:, :],
                                 op=OP.subtract), ("smath",))
            v.wait_ge(s_ac, AC[("rstd",)])
            v.tensor_copy(mstat_bf[:, :], mneg[:, :])
            v.tensor_copy(rstat_bf[:, :], rstdr[:, :])
            vinc(v.drain(), ("scast",))
            for tc in range(4):
                v.wait_ge(s_mm, MM_BCM(tc))
                v.tensor_copy(mbv[:, tc * 512:(tc + 1) * 512], banks[tc][:, :])
            for tc in range(4):
                v.wait_ge(s_mm, MM_BCM(4 + tc))
                v.reciprocal(rbv[:, tc * 512:(tc + 1) * 512], banks[4 + tc][:, :])
            vinc(v.drain(), ("bevac",))
            for h in range(16):
                v.wait_ge(s_gp, h + 1)
                vinc(v.tensor_tensor(out=xn[:, h * T:(h + 1) * T],
                                     in0=t1slot(h)[:, :], in1=rbv[:, :],
                                     op=OP.mult), ("xn", h))
            v.drain()
            def sel_piece(pi):
                j, col, w = PIECES[pi]
                a, b = CHUNKS[j - 1]
                slot = (j - 1) % 2
                ioff = col - a
                v.wait_ge(s_idx[slot], W_IDX(j))
                if col == a and j > 1:
                    v.wait_ge(s_stcc, W_STCC(j - 1))
                for l in range(16):
                    if l == 0:
                        v.tensor_scalar(out=selacc[0][:, 0:w],
                                        in0=selidx[slot][:, ioff:ioff + w],
                                        scalar1=0.0, scalar2=lutb[:, 0:1],
                                        op0=OP.is_equal, op1=OP.mult)
                        v.drain()
                    else:
                        v.tensor_scalar(out=seltmp[:, 0:w],
                                        in0=selidx[slot][:, ioff:ioff + w],
                                        scalar1=float(l), scalar2=lutb[:, l:l + 1],
                                        op0=OP.is_equal, op1=OP.mult)
                        v.drain()
                        v.tensor_tensor(out=selacc[l % 2][:, 0:w],
                                        in0=selacc[1 - (l % 2)][:, 0:w],
                                        in1=seltmp[:, 0:w], op=OP.add)
                        v.drain()
                v.tensor_copy(selw[:, ioff:ioff + w], selacc[1][:, 0:w])
                if CHUNK_CLOSE_PIECE[j] == pi:
                    vinc(v.drain(), ("selchunk", j))
                else:
                    v.drain()
            pi_done = [0]
            for et in range(16):
                for tc in range(4):
                    v.wait_ge(s_mm, MM_V(et, tc))
                    v.wait_ge(s_ac, AC[("kev", et, tc)])
                    vinc(v.scalar_tensor_tensor(
                        out=wslot(et, tc), in0=banks[4 + tc][:, :], scalar=DENOM,
                        in1=kdsb[et % 2][:, tc * 512:(tc + 1) * 512],
                        op0=OP.mult, op1=OP.mult), ("u", et, tc))
                v.drain()
                if et == 0:
                    v.wait_ge(s_const, W_CONST)
                vinc(v.tensor_tensor_scan(
                    out=wslot(et - 1), data0=wslot(et), data1=wslot(et),
                    initial=0.0, op0=OP.add, op1=OP.bypass), ("scan", et))
                v.drain()
                v.tensor_copy(s16[:, et:et + 1], wslot(et - 1)[:, T - 1:T])
                if et == 7:
                    vinc(v.drain(), ("s16a",))
                elif et == 15:
                    vinc(v.drain(), ("s16b",))
                else:
                    v.drain()
                while pi_done[0] < PIECE_CUM[et]:
                    sel_piece(pi_done[0])
                    pi_done[0] += 1
            v.wait_ge(s_ccsa, 16 * 8)
            for r in range(NC):
                v.tensor_scalar(out=ctmp[:, r * 8:(r + 1) * 8],
                                in0=sg[:, r * 16:r * 16 + 8],
                                scalar1=maskb[:, r:r + 1], scalar2=None,
                                op0=OP.mult)
            v.drain()
            for r in range(4):
                v.tensor_tensor(out=ctmp[:, r * 8:(r + 1) * 8],
                                in0=ctmp[:, r * 8:(r + 1) * 8],
                                in1=ctmp[:, (r + 4) * 8:(r + 5) * 8], op=OP.add)
            v.drain()
            for r in range(2):
                v.tensor_tensor(out=ctmp[:, r * 8:(r + 1) * 8],
                                in0=ctmp[:, r * 8:(r + 1) * 8],
                                in1=ctmp[:, (r + 2) * 8:(r + 3) * 8], op=OP.add)
            v.drain()
            v.tensor_tensor(out=ctmp[:, 64:72], in0=ctmp[:, 0:8],
                            in1=ctmp[:, 8:16], op=OP.add)
            vinc(v.drain(), ("carrya",))
            def carry_col(et):
                return ctmp[:, 64 + et:65 + et] if et < 8 else ctmp[:, 72 + et - 8:73 + et - 8]
            for et in range(16):
                if et == 8:
                    v.wait_ge(s_ccsb, 16 * 8)
                    for r in range(NC):
                        v.tensor_scalar(out=ctmp[:, r * 8:(r + 1) * 8],
                                        in0=sg[:, r * 16 + 8:r * 16 + 16],
                                        scalar1=maskb[:, r:r + 1], scalar2=None,
                                        op0=OP.mult)
                    v.drain()
                    for r in range(4):
                        v.tensor_tensor(out=ctmp[:, r * 8:(r + 1) * 8],
                                        in0=ctmp[:, r * 8:(r + 1) * 8],
                                        in1=ctmp[:, (r + 4) * 8:(r + 5) * 8],
                                        op=OP.add)
                    v.drain()
                    for r in range(2):
                        v.tensor_tensor(out=ctmp[:, r * 8:(r + 1) * 8],
                                        in0=ctmp[:, r * 8:(r + 1) * 8],
                                        in1=ctmp[:, (r + 2) * 8:(r + 3) * 8],
                                        op=OP.add)
                    v.drain()
                    v.tensor_tensor(out=ctmp[:, 72:80], in0=ctmp[:, 0:8],
                                    in1=ctmp[:, 8:16], op=OP.add)
                    vinc(v.drain(), ("carryb",))
                for tc in range(4):
                    v.wait_ge(s_mm, MM_Q(et, tc))
                    qb = banks[tc + 4 * (et % 2)]
                    vinc(v.scalar_tensor_tensor(
                        out=wslot(vslot(et), tc), in0=wslot(et - 1, tc),
                        scalar=carry_col(et), in1=qb[:, :],
                        op0=OP.add, op1=OP.mult), ("vhat", et, tc))
                v.drain()
                if 2 <= et <= 13 and pi_done[0] < NPIECE:
                    sel_piece(pi_done[0])
                    pi_done[0] += 1
            for ft in range(16):
                for tc in range(4):
                    v.wait_ge(s_mm, MM_WO(ft, tc))
                    v.wait_ge(s_xr, W_XRL(ft))
                    if ft >= 1:
                        v.wait_ge(s_sty, W_YA(ft - 1))
                    b = banks[tc + 4 * (ft % 2)]
                    vinc(v.tensor_tensor(
                        out=ystgf[:, tc * 512:(tc + 1) * 512],
                        in0=b[:, :], in1=xrl_wo(ft)[:, tc * 512:(tc + 1) * 512],
                        op=OP.add), ("woev", ft, tc))
                v.drain()
                if ft < 4 and pi_done[0] < NPIECE:
                    sel_piece(pi_done[0])
                    pi_done[0] += 1
            for s, sw in enumerate(SWEEPS):
                for i, ot in enumerate(sw):
                    for st in (1, 2):
                        base = 0 if st == 1 else 4
                        v.wait_ge(s_ac, AC[("silu", ot, st)])
                        v.wait_ge(s_mm, MM_FFN[("u%d" % st, ot)])
                        for k in range(2):
                            tc = (st - 1) * 2 + k
                            inst = v.tensor_tensor(
                                out=wslot(i, tc),
                                in0=sgst[:, (st - 1) * 1024 + k * 512:
                                         (st - 1) * 1024 + (k + 1) * 512],
                                in1=banks[base + 2 + k][:, :], op=OP.mult)
                            if k == 1:
                                vinc(inst, ("hm", ot, st))
                        v.drain()
                for ft in range(16):
                    for tc in range(4):
                        v.wait_ge(s_mm, MM_FFN[("d", s, ft, tc)])
                        v.wait_ge(s_xr, W_YRL(s, ft))
                        if ft >= 1:
                            v.wait_ge(s_sty, W_YD(s, ft - 1))
                        elif s > 0:
                            v.wait_ge(s_sty, W_YD(s - 1, 15))
                        else:
                            v.wait_ge(s_sty, W_YA(15))
                        b = banks[tc + 4 * (ft % 2)]
                        vinc(v.tensor_tensor(
                            out=ystgf[:, tc * 512:(tc + 1) * 512],
                            in0=b[:, :],
                            in1=xrl_ffn(ft)[:, tc * 512:(tc + 1) * 512],
                            op=OP.add), ("dev", s, ft, tc))
                    v.drain()

        @block.scalar
        def _(sc):
            cnt = [0]
            def ainc(inst, key):
                cnt[0] += 1
                assert AC[key] == cnt[0], (key, AC[key], cnt[0])
                inst.then_inc(s_ac, 1)
            sc.wait_ge(s_const, 16)
            sc.dma_start(out=fl_in[:, :], in_=lutb[0:1, 0:16]).then_inc(s_flst, 16)
            for h in range(16):
                sc.wait_ge(s_xf[h % 4], W_XF(h))
                if h >= 4:
                    sc.wait_ge(s_mm, MM_STAT(h - 4))
                ainc(sc.activation(sqb[h % 4][:, :], xftv[h % 4][:, :], AF.Square),
                     ("sq", h))
                sc.drain()
            sc.wait_ge(s_ve, VE[("smath",)])
            ainc(sc.activation(rstdr[:, :], varr[:, :], AF.Sqrt), ("rstd",))
            def idx_load(j):
                aa, bb = CHUNKS[j - 1]
                if j > 2:
                    sc.wait_ge(s_ve, VE[("selchunk", j - 2)])
                else:
                    sc.wait_ge(s_ve, VE[("xn", 15)])
                sc.dma_start(out=selidx[(j - 1) % 2][:, 0:bb - aa],
                             in_=idx_h[:, aa:bb]).then_inc(s_idx[(j - 1) % 2], 16)
            idx_load(1)
            idx_load(2)
            pi_done = 0
            for et in range(16):
                if et == 8:
                    sc.wait_ge(s_ve, VE[("s16a",)])
                    sc.dma_start(out=cs_in_a[:, :],
                                 in_=s16[:, 0:8]).then_inc(s_sts, 16)
                for tc in range(4):
                    sc.wait_ge(s_mm, MM_K(et, tc))
                    if et >= 2:
                        sc.wait_ge(s_ve, VE[("u", et - 2, 3)])
                    ainc(sc.copy(kdsb[et % 2][:, tc * 512:(tc + 1) * 512],
                                 banks[tc][:, :]), ("kev", et, tc))
                while pi_done < PIECE_CUM[et]:
                    j = PIECES[pi_done][0]
                    if CHUNK_CLOSE_PIECE[j] == pi_done:
                        a, b = CHUNKS[j - 1]
                        sc.wait_ge(s_ve, VE[("selchunk", j)])
                        sc.dma_start(out=cin[j - 1][:, :],
                                     in_=selw[:, 0:b - a]).then_inc(s_stcc, 16)
                        if j + 2 <= 7:
                            idx_load(j + 2)
                    pi_done += 1
            sc.wait_ge(s_ve, VE[("s16b",)])
            sc.dma_start(out=cs_in_b[:, :], in_=s16[:, 8:16]).then_inc(s_sts, 16)
            sc.wait_ge(s_cc, CC_DONE["ca"])
            for r in range(NC):
                sc.dma_start(out=sg[:, r * 16:r * 16 + 8],
                             in_=cs_out_a[r, :, :]).then_inc(s_ccsa, 16)
            sc.wait_ge(s_ve, VE[("vhat", 0, 3)])
            sc.dma_start(out=xrlA[:, :], in_=xft_h[0, :, :]).then_inc(s_xr, 16)
            sc.wait_ge(s_cc, CC_DONE["cb"])
            for r in range(NC):
                sc.dma_start(out=sg[:, r * 16 + 8:r * 16 + 16],
                             in_=cs_out_b[r, :, :]).then_inc(s_ccsb, 16)
            idx_load(8)
            sc.wait_ge(s_ve, VE[("selchunk", 6)])
            sc.dma_start(out=cin[5][:, :],
                         in_=selw[:, 0:4096]).then_inc(s_stcc, 16)
            sc.wait_ge(s_ve, VE[("selchunk", 7)])
            sc.dma_start(out=cin[6][:, :],
                         in_=selw[:, 0:5504]).then_inc(s_stcc, 16)
            sc.wait_ge(s_ve, VE[("vhat", 15, 3)])
            sc.dma_start(out=xrlB[:, :], in_=xft_h[1, :, :]).then_inc(s_xr, 16)
            for ft in range(16):
                sc.wait_ge(s_ve, VE[("woev", ft, 3)])
                sc.dma_start(out=y_h[ft, :, :], in_=ystgf[:, :]).then_inc(s_sty, 16)
                if ft + 2 <= 15:
                    sc.dma_start(out=xrl_wo(ft + 2)[:, :],
                                 in_=xft_h[ft + 2, :, :]).then_inc(s_xr, 16)
                if ft == 4:
                    sc.wait_ge(s_ve, VE[("selchunk", 8)])
                    sc.dma_start(out=cin[7][:, :],
                                 in_=selw[:, 0:5504]).then_inc(s_stcc, 16)
            for s, sw in enumerate(SWEEPS):
                for i, ot in enumerate(sw):
                    for st in (1, 2):
                        base = 0 if st == 1 else 4
                        sc.wait_ge(s_mm, MM_FFN[("g%d" % st, ot)])
                        if ot > 0:
                            sc.wait_ge(s_ve, VE[("hm", ot - 1, st)])
                        for k in range(2):
                            inst = sc.activation(
                                sgst[:, (st - 1) * 1024 + k * 512:
                                     (st - 1) * 1024 + (k + 1) * 512],
                                banks[base + k][:, :], AF.Silu)
                            if k == 1:
                                ainc(inst, ("silu", ot, st))
                        sc.drain()
                for ft in range(16):
                    sc.wait_ge(s_ve, VE[("dev", s, ft, 3)])
                    sc.dma_start(out=y_h[ft, :, :],
                                 in_=ystgf[:, :]).then_inc(s_sty, 16)
            sc.wait_ge(s_sty, STY_FINAL)

        @block.gpsimd
        def _(g):
            for et in range(2):
                g.dma_start(out=wkb_d[et, :, :],
                            in_=wk_h[et, :, :]).then_inc(s_ck, 16)
                g.dma_start(out=wvb_d[et, :, :],
                            in_=wv_h[et, :, :]).then_inc(s_cv, 16)
            for h in range(16):
                if h == 0:
                    g.wait_ge(s_ve, VE[("bevac",)])
                if h >= 2:
                    g.wait_ge(s_ve, VE[("xn", h - 2)])
                g.wait_ge(s_xf[h % 2], W_XF2(h))
                g.tensor_tensor(out=t1slot(h)[:, :],
                                in0=xftv[h % 2][:, :], in1=mbv[:, :],
                                op=OP.add).then_inc(s_gp, 1)
            for et in range(2, 16):
                g.dma_start(out=wkb_d[et, :, :],
                            in_=wk_h[et, :, :]).then_inc(s_ck, 16)
                g.dma_start(out=wvb_d[et, :, :],
                            in_=wv_h[et, :, :]).then_inc(s_cv, 16)
            for et in range(16):
                g.dma_start(out=wqb_d[et, :, :],
                            in_=wq_h[et, :, :]).then_inc(s_cq, 16)
            for et in range(16):
                g.dma_start(out=wob_d[et, :, :],
                            in_=wo_h[et, :, :]).then_inc(s_co, 16)
            def coll(inbuf, outbuf):
                g.collective_compute(
                    "AllGather", OP.bypass,
                    replica_groups=[list(range(NC))],
                    ins=[inbuf], outs=[outbuf]).then_inc(s_cc, 1)
            g.wait_ge(s_stcc, W_STCC(1))
            coll(cin[0][:, :], cout[0][:, :, :])
            g.wait_ge(s_stcc, W_STCC(2))
            coll(cin[1][:, :], cout[1][:, :, :])
            g.wait_ge(s_stcc, W_STCC(3))
            coll(cin[2][:, :], cout[2][:, :, :])
            g.wait_ge(s_sts, 16)
            coll(cs_in_a[:, :], cs_out_a[:, :, :])
            g.wait_ge(s_flst, 16)
            coll(fl_in[:, :], fl_out[0][:, :, :])
            g.wait_ge(s_stcc, W_STCC(4))
            coll(cin[3][:, :], cout[3][:, :, :])
            g.wait_ge(s_sts, 32)
            coll(cs_in_b[:, :], cs_out_b[:, :, :])
            coll(fl_in[:, :], fl_out[1][:, :, :])
            g.wait_ge(s_stcc, W_STCC(5))
            coll(cin[4][:, :], cout[4][:, :, :])
            g.wait_ge(s_stcc, W_STCC(6))
            coll(cin[5][:, :], cout[5][:, :, :])
            g.wait_ge(s_stcc, W_STCC(7))
            coll(cin[6][:, :], cout[6][:, :, :])
            g.wait_ge(s_stcc, W_STCC(8))
            coll(cin[7][:, :], cout[7][:, :, :])
            coll(fl_in[:, :], fl_out[2][:, :, :])
            for s in range(3):
                for ft in range(16):
                    if s == 0 and ft == 0:
                        g.wait_ge(s_ve, VE[("woev", 14, 3)])
                        g.wait_ge(s_sty, W_YA(0))
                    elif s == 0 and ft == 1:
                        g.wait_ge(s_mm, MM_WO_END)
                        g.wait_ge(s_sty, W_YA(1))
                    else:
                        if ft >= 2:
                            g.wait_ge(s_ve, VE[("dev", s, ft - 2, 3)])
                        else:
                            g.wait_ge(s_ve, VE[("dev", s - 1, 14 + ft, 3)])
                        g.wait_ge(s_sty,
                                  W_YA(ft) if s == 0 else W_YD(s - 1, ft))
                    g.dma_start(out=xrl_ffn(ft)[:, :],
                                in_=y_h[ft, :, :]).then_inc(s_xr, 16)

        @block.sync
        def _(sy):
            sy.dma_start(out=lutb[:, :], in_=lutb_h[:, :]).then_inc(s_const, 16)
            sy.dma_start(out=maskb[:, :], in_=mask_h[:, :]).then_inc(s_const, 16)
            for h in range(16):
                if h >= 4:
                    sy.wait_ge(s_ve, VE[("cast", h - 4)])
                    sy.wait_ge(s_ac, AC[("sq", h - 4)])
                sy.dma_start(out=xftv[h % 4][:, :],
                             in_=xft_h[h, :, :]).then_inc(s_xf[h % 4], 16)
            for et in range(2):
                sy.wait_ge(s_ck, 16 * (et + 1))
                sy.dma_start(out=wkvb[:, (et % 2) * D:(et % 2 + 1) * D],
                             in_=wkb_d[et, :, :]).then_inc(s_k[et % 2], 16)
                sy.wait_ge(s_cv, 16 * (et + 1))
                sy.dma_start(out=wkvb[:, (2 + et % 2) * D:(3 + et % 2) * D],
                             in_=wvb_d[et, :, :]).then_inc(s_v[et % 2], 16)
            for h in range(16):
                if h < 2:
                    sy.wait_ge(s_ve, VE[("cast", 12 + h)])
                    sy.wait_ge(s_ac, AC[("sq", 12 + h)])
                else:
                    sy.wait_ge(s_gp, h - 1)
                sy.dma_start(out=xftv[h % 2][:, :],
                             in_=xft_h[h, :, :]).then_inc(s_xf[h % 2], 16)
            for et in range(2, 16):
                sy.wait_ge(s_ck, 16 * (et + 1))
                sy.wait_ge(s_mm, MM_K(et - 2, 3))
                sy.dma_start(out=wkvb[:, (et % 2) * D:(et % 2 + 1) * D],
                             in_=wkb_d[et, :, :]).then_inc(s_k[et % 2], 16)
                sy.wait_ge(s_cv, 16 * (et + 1))
                sy.wait_ge(s_mm, MM_V(et - 2, 3))
                sy.dma_start(out=wkvb[:, (2 + et % 2) * D:(3 + et % 2) * D],
                             in_=wvb_d[et, :, :]).then_inc(s_v[et % 2], 16)
            for et in range(16):
                sy.wait_ge(s_cq, 16 * (et + 1))
                if et < 2:
                    sy.wait_ge(s_mm, MM_K(14 + et, 3))
                else:
                    sy.wait_ge(s_mm, MM_Q(et - 2, 3))
                sy.dma_start(out=wkvb[:, (et % 2) * D:(et % 2 + 1) * D],
                             in_=wqb_d[et, :, :]).then_inc(s_k[et % 2], 16)
            for ft in range(16):
                sy.wait_ge(s_co, 16 * (ft + 1))
                if ft < 2:
                    sy.wait_ge(s_mm, MM_V(14 + ft, 3))
                else:
                    sy.wait_ge(s_mm, MM_WO(ft - 2, 3))
                sy.dma_start(out=wkvb[:, (2 + ft % 2) * D:(3 + ft % 2) * D],
                             in_=wob_d[ft, :, :]).then_inc(s_v[ft % 2], 16)
            for s, sw in enumerate(SWEEPS):
                for ot in sw:
                    rg, colg = _gate_loc(ot)
                    ru, colu = _up_loc(ot)
                    jg, ju = _chunk_of_col(colg), _chunk_of_col(colu)
                    ag = CHUNKS[jg - 1][0]
                    au = CHUNKS[ju - 1][0]
                    sy.wait_ge(s_cc, CC_DONE[jg])
                    if ot < 2:
                        sy.wait_ge(s_stcc, W_STCC(8))
                    else:
                        sy.wait_ge(s_mm, MM_FFN[("g2", ot - 2)])
                    sy.dma_start(out=gbuf[ot % 2][:, :],
                                 in_=cout[jg - 1][rg, :, colg - ag:colg - ag + 2048]
                                 ).then_inc(s_g[ot % 2], 16)
                    sy.wait_ge(s_cc, CC_DONE[ju])
                    if ot < 2:
                        sy.wait_ge(s_stcc, W_STCC(8))
                    else:
                        sy.wait_ge(s_mm, MM_FFN[("u2", ot - 2)])
                    sy.dma_start(out=ubuf[ot % 2][:, :],
                                 in_=cout[ju - 1][ru, :, colu - au:colu - au + 2048]
                                 ).then_inc(s_u[ot % 2], 16)
                nsw = len(sw)
                it0 = sw[0]
                for ft in range(16):
                    rd, cold = _down_loc(ft)
                    jd = _chunk_of_col(cold)
                    ad = CHUNKS[jd - 1][0]
                    sy.wait_ge(s_cc, CC_DONE[jd])
                    if s == 0 and ft < 2:
                        sy.wait_ge(s_ve, VE[("u", 15, 3)])
                        sy.wait_ge(s_stcc, W_STCC(8))
                    if ft >= 2:
                        sy.wait_ge(s_mm, MM_FFN[("d", s, ft - 2, 3)])
                    elif s > 0:
                        sy.wait_ge(s_mm, MM_FFN[("d", s - 1, 14 + ft, 3)])
                    off = cold - ad + it0 * 128
                    sy.dma_start(out=dbuf[ft % 2][:, 0:nsw * 128],
                                 in_=cout[jd - 1][rd, :, off:off + nsw * 128]
                                 ).then_inc(s_d[ft % 2], 16)

        @block.tensor
        def _(te):
            cnt = [0]
            def minc(inst, val):
                cnt[0] += 1
                assert val == cnt[0], (val, cnt[0])
                inst.then_inc(s_mm, 1)
            te.wait_ge(s_ve, VE[("init",)])
            for h in range(16):
                te.wait_ge(s_ve, VE[("cast", h)])
                for tc in range(4):
                    te.matmul(banks[tc][0:1, :], onesb[:, 0:1],
                              xn[:, h * T + tc * 512:h * T + (tc + 1) * 512],
                              start=(h == 0), stop=(h == 15))
                te.wait_ge(s_ac, AC[("sq", h)])
                last = None
                for tc in range(4):
                    last = te.matmul(banks[4 + tc][0:1, :], onesb[:, 0:1],
                                     sqb[h % 4][:, tc * 512:(tc + 1) * 512],
                                     start=(h == 0), stop=(h == 15))
                minc(last, MM_STAT(h))
            te.wait_ge(s_ve, VE[("scast",)])
            for tc in range(4):
                minc(te.matmul(banks[tc][:, :], onesb[0:1, 0:128],
                               mstat_bf[:, tc * 512:(tc + 1) * 512],
                               start=True, stop=True), MM_BCM(tc))
            for tc in range(4):
                minc(te.matmul(banks[4 + tc][:, :], onesb[0:1, 0:128],
                               rstat_bf[:, tc * 512:(tc + 1) * 512],
                               start=True, stop=True), MM_BCM(4 + tc))
            te.wait_ge(s_ve, VE[("bevac",)])
            te.wait_ge(s_k[0], W_K(0))
            te.wait_ge(s_v[0], W_V(0))
            lastk, lastv = {}, {}
            for dd in range(16):
                te.wait_ge(s_ve, VE[("xn", dd)])
                for tc in range(4):
                    lastk[tc] = te.matmul(
                        banks[tc][:, :],
                        wkvb[:, dd * 128:(dd + 1) * 128],
                        xn[:, dd * T + tc * 512:dd * T + (tc + 1) * 512],
                        start=(dd == 0), stop=(dd == 15))
                for tc in range(4):
                    lastv[tc] = te.matmul(
                        banks[4 + tc][:, :],
                        wkvb[:, 2 * D + dd * 128:2 * D + (dd + 1) * 128],
                        xn[:, dd * T + tc * 512:dd * T + (tc + 1) * 512],
                        start=(dd == 0), stop=(dd == 15))
            for tc in range(4):
                minc(lastk[tc], MM_K(0, tc))
            for tc in range(4):
                minc(lastv[tc], MM_V(0, tc))
            for et in range(1, 16):
                te.wait_ge(s_k[et % 2], W_K(et))
                for tc in range(4):
                    te.wait_ge(s_ac, AC[("kev", et - 1, tc)])
                last = {}
                for dd in range(16):
                    for tc in range(4):
                        last[tc] = te.matmul(
                            banks[tc][:, :],
                            wkvb[:, (et % 2) * D + dd * 128:(et % 2) * D + (dd + 1) * 128],
                            xn[:, dd * T + tc * 512:dd * T + (tc + 1) * 512],
                            start=(dd == 0), stop=(dd == 15))
                for tc in range(4):
                    minc(last[tc], MM_K(et, tc))
                te.wait_ge(s_v[et % 2], W_V(et))
                for tc in range(4):
                    te.wait_ge(s_ve, VE[("u", et - 1, tc)])
                for dd in range(16):
                    for tc in range(4):
                        last[tc] = te.matmul(
                            banks[4 + tc][:, :],
                            wkvb[:, (2 + et % 2) * D + dd * 128:(2 + et % 2) * D + (dd + 1) * 128],
                            xn[:, dd * T + tc * 512:dd * T + (tc + 1) * 512],
                            start=(dd == 0), stop=(dd == 15))
                for tc in range(4):
                    minc(last[tc], MM_V(et, tc))
            for et in range(16):
                te.wait_ge(s_k[et % 2], W_Q(et))
                if et == 0:
                    for tc in range(4):
                        te.wait_ge(s_ac, AC[("kev", 15, tc)])
                elif et == 1:
                    for tc in range(4):
                        te.wait_ge(s_ve, VE[("u", 15, tc)])
                else:
                    for tc in range(4):
                        te.wait_ge(s_ve, VE[("vhat", et - 2, tc)])
                last = {}
                for dd in range(16):
                    for tc in range(4):
                        last[tc] = te.matmul(
                            banks[tc + 4 * (et % 2)][:, :],
                            wkvb[:, (et % 2) * D + dd * 128:(et % 2) * D + (dd + 1) * 128],
                            xn[:, dd * T + tc * 512:dd * T + (tc + 1) * 512],
                            start=(dd == 0), stop=(dd == 15))
                for tc in range(4):
                    minc(last[tc], MM_Q(et, tc))
            te.wait_ge(s_ve, VE[("vhat", 15, 3)])
            for ft in range(16):
                te.wait_ge(s_v[ft % 2], W_WOC(ft))
                if ft < 2:
                    for tc in range(4):
                        te.wait_ge(s_ve, VE[("vhat", 14 + ft, tc)])
                else:
                    for tc in range(4):
                        te.wait_ge(s_ve, VE[("woev", ft - 2, tc)])
                last = {}
                for ee in range(16):
                    for tc in range(4):
                        last[tc] = te.matmul(
                            banks[tc + 4 * (ft % 2)][:, :],
                            wkvb[:, (2 + ft % 2) * D + ee * 128:(2 + ft % 2) * D + (ee + 1) * 128],
                            wslot(vslot(ee), tc),
                            start=(ee == 0), stop=(ee == 15))
                for tc in range(4):
                    minc(last[tc], MM_WO(ft, tc))
            for s, sw in enumerate(SWEEPS):
                for i, ot in enumerate(sw):
                    te.wait_ge(s_g[ot % 2], W_G(ot))
                    te.wait_ge(s_u[ot % 2], W_U(ot))
                    if s == 0 and i == 0:
                        te.wait_ge(s_ve, VE[("woev", 14, 3)])
                    elif i == 0:
                        te.wait_ge(s_ve, VE[("dev", s - 1, 14, 3)])
                    else:
                        te.wait_ge(s_ac, AC[("silu", ot - 1, 1)])
                        te.wait_ge(s_ve, VE[("hm", ot - 1, 1)])
                    lastg, lastu = {}, {}
                    for dd in range(16):
                        for k in range(2):
                            lastg[k] = te.matmul(
                                banks[k][:, :],
                                gbuf[ot % 2][:, dd * 128:(dd + 1) * 128],
                                xn[:, dd * T + k * 512:dd * T + (k + 1) * 512],
                                start=(dd == 0), stop=(dd == 15))
                        for k in range(2):
                            lastu[k] = te.matmul(
                                banks[2 + k][:, :],
                                ubuf[ot % 2][:, dd * 128:(dd + 1) * 128],
                                xn[:, dd * T + k * 512:dd * T + (k + 1) * 512],
                                start=(dd == 0), stop=(dd == 15))
                    minc(lastg[1], MM_FFN[("g1", ot)])
                    minc(lastu[1], MM_FFN[("u1", ot)])
                    if i > 0:
                        te.wait_ge(s_ac, AC[("silu", ot - 1, 2)])
                        te.wait_ge(s_ve, VE[("hm", ot - 1, 2)])
                    elif s == 0:
                        te.wait_ge(s_ve, VE[("woev", 15, 3)])
                    else:
                        te.wait_ge(s_ve, VE[("dev", s - 1, 15, 3)])
                    for dd in range(16):
                        for k in range(2):
                            lastg[k] = te.matmul(
                                banks[4 + k][:, :],
                                gbuf[ot % 2][:, dd * 128:(dd + 1) * 128],
                                xn[:, dd * T + (2 + k) * 512:dd * T + (3 + k) * 512],
                                start=(dd == 0), stop=(dd == 15))
                        for k in range(2):
                            lastu[k] = te.matmul(
                                banks[6 + k][:, :],
                                ubuf[ot % 2][:, dd * 128:(dd + 1) * 128],
                                xn[:, dd * T + (2 + k) * 512:dd * T + (3 + k) * 512],
                                start=(dd == 0), stop=(dd == 15))
                    minc(lastg[1], MM_FFN[("g2", ot)])
                    minc(lastu[1], MM_FFN[("u2", ot)])
                nsw = len(sw)
                te.wait_ge(s_ve, VE[("hm", sw[-1], 2)])
                for ft in range(16):
                    te.wait_ge(s_d[ft % 2], W_D(s, ft))
                    if ft == 0:
                        te.wait_ge(s_ac, AC[("silu", sw[-1], 1)])
                    elif ft == 1:
                        te.wait_ge(s_ac, AC[("silu", sw[-1], 2)])
                    else:
                        for tc in range(4):
                            te.wait_ge(s_ve, VE[("dev", s, ft - 2, tc)])
                    last = {}
                    for iti in range(nsw):
                        for tc in range(4):
                            last[tc] = te.matmul(
                                banks[tc + 4 * (ft % 2)][:, :],
                                dbuf[ft % 2][:, iti * 128:(iti + 1) * 128],
                                wslot(iti, tc),
                                start=(iti == 0), stop=(iti == nsw - 1))
                    for tc in range(4):
                        minc(last[tc], MM_FFN[("d", s, ft, tc)])

    return nc


def _blk(W):
    return np.ascontiguousarray(
        W.reshape(16, 128, 16, 128).transpose(0, 3, 2, 1).reshape(16, 128, D))


def kernel(L0_attn_x, L0_attn_Wq, L0_attn_Wk, L0_attn_Wv, L0_attn_Wo,
           L0_ffn_lut, L0_ffn_gate_idx, L0_ffn_up_idx, L0_ffn_down_idx):
    x = np.asarray(L0_attn_x, dtype=np.float32)
    Wq = np.asarray(L0_attn_Wq, dtype=np.float32)
    Wk = np.asarray(L0_attn_Wk, dtype=np.float32)
    Wv = np.asarray(L0_attn_Wv, dtype=np.float32)
    Wo = np.asarray(L0_attn_Wo, dtype=np.float32)
    lut = np.asarray(L0_ffn_lut, dtype=np.float32)
    gidx = np.asarray(L0_ffn_gate_idx)
    uidx = np.asarray(L0_ffn_up_idx)
    didx = np.asarray(L0_ffn_down_idx)

    wqb, wkb, wvb, wob = _blk(Wq), _blk(Wk), _blk(Wv), _blk(Wo)
    lutb = np.ascontiguousarray(np.broadcast_to(lut[None, :], (128, 16)))

    g_can = gidx.reshape(43, 128, 16, 128).transpose(0, 3, 2, 1).reshape(43, 128, 2048)
    u_can = uidx.reshape(43, 128, 16, 128).transpose(0, 3, 2, 1).reshape(43, 128, 2048)
    d_can = didx.reshape(16, 128, 43, 128).transpose(0, 3, 2, 1).reshape(16, 128, 5504)

    in_maps = []
    for c in range(NC):
        bb, jj = c // 4, c % 4
        xs = x[bb, jj * T:(jj + 1) * T, :]
        xft = np.ascontiguousarray(xs.T.reshape(16, 128, T))
        reg = np.zeros((128, REG_COLS), np.float32)
        a0, b0 = G_OWN[c]
        for s, ot in enumerate(range(a0, b0)):
            reg[:, s * 4096:s * 4096 + 2048] = g_can[ot]
            reg[:, s * 4096 + 2048:(s + 1) * 4096] = u_can[ot]
        for s, ft in enumerate((2 * c, 2 * c + 1)):
            reg[:, REG_D0 + s * 5504:REG_D0 + (s + 1) * 5504] = d_can[ft]
        regb = reg.astype(ml_dtypes.bfloat16)
        mask = np.zeros((128, 8), np.float32)
        for r in range(NC):
            if r // 4 == bb and (r % 4) < jj:
                mask[:, r] = 1.0
        in_maps.append({
            "xft_h": xft,
            "wq_h": wqb, "wk_h": wkb, "wv_h": wvb, "wo_h": wob,
            "idx_h": regb, "lutb_h": lutb, "mask_h": mask,
        })

    nc = _build_nc()
    res = run_bass_kernel_spmd(nc, in_maps, core_ids=list(range(NC)))
    out = np.empty((2, 8192, D), np.float32)
    for c in range(NC):
        bb, jj = c // 4, c % 4
        yf = np.asarray(res.results[c]["y_h"]).reshape(D, T)
        out[bb, jj * T:(jj + 1) * T, :] = yf.T
    return out
